# version 08
# baseline (speedup 1.0000x reference)
import numpy as np

import concourse.bass as bass
import concourse.mybir as mybir
from concourse.bass_utils import run_bass_kernel_spmd

F32 = mybir.dt.float32
BF16 = mybir.dt.bfloat16
AF = mybir.ActivationFunctionType
ALU = mybir.AluOpType
AX = mybir.AxisListType

P = 128
T = 4096
E = 1024
H = 1024
F4 = 4096
FE = 2048
NE = 8

TCA = 256
NCA = T // TCA
TCF = 512
NCF = T // TCF

LNEPS_OUT = 1e-5
LNEPS_IN = 1e-3
P_INIT = -1e30

_DEBUG = False


class Bld:
    ENGS = ("sync", "gpsimd", "scalar", "vector", "tensor")
    SELF = {"vector": "v", "scalar": "a", "gpsimd": "p"}
    NQSEM = 8

    def __init__(self, nc, sems):
        self.nc = nc
        self.sems = sems
        self.q = {e: [] for e in self.ENGS}
        self.c = {k: 0 for k in ("v", "a", "p", "pe")}
        for base in ("sq", "wq", "gq"):
            self.c[base] = 0
            for i in range(self.NQSEM):
                self.c[f"{base}{i}"] = 0

    def _mk_waits(self, waits):
        out = []
        for w in waits:
            if w is None:
                continue
            (s, val) = w
            if val > 0:
                out.append((self.sems[s], val))
        return out

    def op(self, eng, fn, waits=()):
        sc = self.SELF.get(eng)
        w = self._mk_waits(waits)
        if sc is not None:
            k = self.c[sc]
            sem = self.sems[sc]

            def run(e, fn=fn, w=w, k=k, sem=sem):
                for (s, val) in w:
                    e.wait_ge(s, val)
                if k > 0:
                    e.wait_ge(sem, k)
                fn(e).then_inc(sem, 1)

            self.q[eng].append(run)
            self.c[sc] = k + 1
            return (sc, k + 1)
        else:
            def run(e, fn=fn, w=w):
                for (s, val) in w:
                    e.wait_ge(s, val)
                fn(e)

            self.q[eng].append(run)
            return None

    def mm(self, fn, waits=(), mark=False):
        w = self._mk_waits(waits)
        if mark:
            self.c["pe"] += 1
            pe_sem = self.sems["pe"]

            def run(e, fn=fn, w=w, pe_sem=pe_sem):
                for (s, val) in w:
                    e.wait_ge(s, val)
                fn(e).then_inc(pe_sem, 1)

            self.q["tensor"].append(run)
            return ("pe", self.c["pe"])
        else:
            def run(e, fn=fn, w=w):
                for (s, val) in w:
                    e.wait_ge(s, val)
                fn(e)

            self.q["tensor"].append(run)
            return None

    def dma(self, queue, fn, waits=()):
        base = {"sync": "sq", "scalar": "wq", "gpsimd": "gq"}[queue]
        idx = self.c[base] % self.NQSEM
        key = f"{base}{idx}"
        sem = self.sems[key]
        prev_done = self.c[key]
        self.c[base] += 1
        self.c[key] = prev_done + 16
        w = self._mk_waits(waits)

        def run(e, fn=fn, w=w, sem=sem, prev_done=prev_done):
            if prev_done > 0:
                e.wait_ge(sem, prev_done)
            for (s, val) in w:
                e.wait_ge(s, val)
            fn(e).then_inc(sem, 16)

        self.q[queue].append(run)
        return (key, self.c[key])

    def queue_done_waits(self, queue):
        base = {"sync": "sq", "scalar": "wq", "gpsimd": "gq"}[queue]
        return [(f"{base}{i}", self.c[f"{base}{i}"]) for i in range(self.NQSEM)
                if self.c.get(f"{base}{i}", 0) > 0]

    def raw(self, eng, fn):
        self.q[eng].append(fn)


def rearr_rows(dram_ap, r0, rows):
    return dram_ap[r0:r0 + rows, :].rearrange("(t p) e -> p t e", p=P)


def build_nc():
    nc = bass.Bass("TRN2", target_bir_lowering=False, debug=False)

    I = {}
    I["x"] = nc.dram_tensor("x", [T, E], F32, kind="ExternalInput").ap()
    for nm, shp in [("ln1g", [E]), ("ln1b", [E]), ("ln2g", [E]), ("ln2b", [E]),
                    ("mixk", [E]), ("mixv", [E]), ("mixr", [E]),
                    ("cmixk", [E]), ("cmixr", [E]), ("td", [H]), ("tf", [H])]:
        I[nm] = nc.dram_tensor(nm, shp, F32, kind="ExternalInput").ap()
    for nm, shp in [("Wk", [E, H]), ("Wv", [E, H]), ("Wr", [E, H]), ("Wo", [H, E]),
                    ("cWk", [E, F4]), ("cWv", [F4, E]), ("cWr", [E, E]),
                    ("rw", [NE, E]), ("w1", [NE, E, FE]), ("w2", [NE, FE, E])]:
        I[nm] = nc.dram_tensor(nm, shp, F32, kind="ExternalInput").ap()

    y = nc.dram_tensor("y", [T, E], F32, kind="ExternalOutput").ap()

    S = {}
    for nm, shp in [("bWk", [E, H]), ("bWv", [E, H]), ("bWr", [E, H]), ("bWo", [H, E]),
                    ("bcWk", [E, F4]), ("bcWv", [F4, E]), ("bcWr", [E, E]),
                    ("bw1", [NE, E, FE]), ("bw2", [NE, FE, E]),
                    ("xn_b", [T, E]), ("g_b", [T, E])]:
        S[nm] = nc.dram_tensor(nm, shp, BF16, kind="Internal").ap()
    S["s_f"] = nc.dram_tensor("s_f", [T, E], F32, kind="Internal").ap()

    dbg = {}
    if _DEBUG:
        for nm, shp in [("d_xn", [T, E]), ("d_k", [H, T]), ("d_wkv", [H, T]),
                        ("d_x1", [T, E]), ("d_s", [T, E]), ("d_g", [T, E])]:
            dbg[nm] = nc.dram_tensor(nm, shp, F32, kind="ExternalOutput").ap()

    def sb(name, shape, dt):
        return nc.alloc_sbuf_tensor(name, shape, dt)

    g1 = sb("g1", [P, 8], F32); b1 = sb("b1", [P, 8], F32)
    g2 = sb("g2", [P, 8], F32); b2 = sb("b2", [P, 8], F32)
    mk = sb("mk", [P, 8], F32); mv = sb("mv", [P, 8], F32); mr = sb("mr", [P, 8], F32)
    omk = sb("omk", [P, 8], F32); omv = sb("omv", [P, 8], F32); omr = sb("omr", [P, 8], F32)
    cmk = sb("cmk", [P, 8], F32); cmr = sb("cmr", [P, 8], F32)
    ocmk = sb("ocmk", [P, 8], F32); ocmr = sb("ocmr", [P, 8], F32)
    uvec = sb("uvec", [P, 8], F32)
    wdec = sb("wdec", [P, 8], F32)
    ew = sb("ew", [P, 8], F32)
    ecw = sb("ecw", [P, 8], F32)
    ew0 = sb("ew0", [P, 8 * TCA], F32)
    ejw = sb("ejw", [P, 8 * TCA], F32)
    aa = sb("aa", [P, 8], F32); bb = sb("bb", [P, 8], F32); pst = sb("pst", [P, 8], F32)
    aag = sb("aag", [P, 8], F32); bbg = sb("bbg", [P, 8], F32)
    ng = sb("ng", [P, 8], F32); ug = sb("ug", [P, 8], F32)
    sc8 = sb("sc8", [P, 8], F32)
    carry1 = sb("carry1", [P, 8], BF16)
    carry2 = sb("carry2", [P, 8], BF16)
    stat = sb("stat", [P, 12], F32)
    epsT = sb("epsT", [P, 2], F32)
    idenB = sb("idenB", [P, 128], BF16)
    onesB = sb("onesB", [P, 128], BF16)
    rwT = sb("rwT", [P, 8, NE], BF16)
    rwTf = sb("rwTf", [P, 8, NE], F32)
    logit = sb("logit", [P, 4, NE], F32)
    wts = sb("wts", [P, 4, NE], F32)
    tkA = sb("tkA", [P, 4, NE], F32)
    tkB = sb("tkB", [P, 4, NE], F32)
    m1n = sb("m1n", [P, 4], F32)
    m2n = sb("m2n", [P, 4], F32)
    zsum = sb("zsum", [P, 4], F32)

    x0c = sb("x0c", [P, 2, E], F32)[:]
    xnc = sb("xnc", [P, 2, E], F32)[:]
    kT = sb("kT", [P, 8, TCA], F32)[:]
    ecb = sb("ecb", [P, 8, TCA], F32)[:]
    zab = sb("zab", [P, 8, TCA], F32)[:]
    sab = sb("sab", [P, 8, TCA], F32)[:]
    relA = sb("relA", [P, TCA], F32)[:]
    relB = sb("relB", [P, TCA], F32)[:]
    tmp5 = sb("tmp5", [P, 512], F32)[:]
    t1b = sb("t1b", [P, 2, E], BF16)[:]
    hT = sb("hT", [P, 8, TCA], BF16)[:]
    hsh = sb("hsh", [P, 8, TCA], BF16)[:]
    xm0 = sb("xm0", [P, 8, TCA], BF16)[:]
    xm1 = sb("xm1", [P, 8, TCA], BF16)[:]
    vT = sb("vT", [P, 8, TCA], BF16)[:]
    rc2 = sb("rc2", [P, 2, E], BF16)[:]
    kcH = sb("kcH", [P, 16, TCA], BF16)[:]
    wtile = [sb(f"wtile{i}", [P, 8, 128], BF16) for i in range(4)]
    wrow = [sb(f"wrow{i}", [P, 512], BF16) for i in range(4)]

    gacc = sb("gacc", [P, 4, E], F32)[:]
    xnT = sb("xnT", [P, 8, TCF], BF16)[:]
    heT = sb("heT", [P, 8, TCF], BF16)[:]
    w1t = [sb(f"w1t{i}", [P, 8, 128], BF16) for i in range(4)]
    w2s = [sb(f"w2s{i}", [P, 8, H], BF16) for i in range(2)]
    gout = sb("gout", [P, 4, E], BF16)[:]

    PS = [nc.alloc_psum_tensor(f"ps{i}", [P, 512], F32) for i in range(8)]
    FB = [0, 1]
    FC_ = [2, 3]
    AB = [4, 5]
    AP_ = [6, 7]

    sems = {}
    sem_cms = []
    sem_names = ["v", "a", "p", "pe"]
    for base in ("sq", "wq", "gq"):
        for i in range(Bld.NQSEM):
            sem_names.append(f"{base}{i}")
    for k in sem_names:
        cm = nc.semaphore(f"sem_{k}")
        sems[k] = cm.__enter__()
        sem_cms.append(cm)

    b = Bld(nc, sems)
    psum_free = {i: None for i in range(8)}
    flat = lambda ap3: ap3.rearrange("p a b -> p (a b)")

    cast_done = {}
    for src, dst in [("Wk", "bWk"), ("Wv", "bWv"), ("Wr", "bWr"), ("Wo", "bWo"),
                     ("cWr", "bcWr"), ("cWk", "bcWk"), ("cWv", "bcWv")]:
        cast_done[dst] = b.dma("gpsimd", lambda e, s_=I[src], d=S[dst]: e.dma_start(d, s_))
    expert_casts = [(ei, kind) for ei in range(NE) for kind in ("bw1", "bw2")]

    def vec_load(dst, src):
        return b.dma("sync", lambda e, d=dst, s_=src: e.dma_start(
            d[:], s_.rearrange("(i p) -> p i", p=P), allow_slow_non_contiguous=True))

    pv = []
    for dst, src in [(g1, "ln1g"), (b1, "ln1b"), (g2, "ln2g"), (b2, "ln2b"),
                     (mk, "mixk"), (mv, "mixv"), (mr, "mixr"),
                     (cmk, "cmixk"), (cmr, "cmixr"), (uvec, "tf"), (sc8, "td")]:
        pv.append(vec_load(dst, I[src]))
    rw_load = b.dma("sync", lambda e: e.dma_start(
        rwTf[:], I["rw"].rearrange("e (i p) -> p e i", p=P),
        allow_slow_non_contiguous=True))

    b.op("vector", lambda e: e.memset(epsT[:, 0:1], LNEPS_OUT), waits=pv + [rw_load])
    b.op("vector", lambda e: e.memset(epsT[:, 1:2], LNEPS_IN))
    for mvv, om in ((mk, omk), (mv, omv), (mr, omr), (cmk, ocmk), (cmr, ocmr)):
        b.op("vector", lambda e, mvv=mvv, om=om: e.tensor_scalar(
            om[:], mvv[:], -1.0, 1.0, ALU.mult, ALU.add))
    m_exp_td = b.op("scalar", lambda e: e.activation(wdec[:], sc8[:], AF.Exp), waits=pv)
    m_neg = b.op("vector", lambda e: e.tensor_scalar(wdec[:], wdec[:], -1.0, None, ALU.mult),
                 waits=[m_exp_td])
    m_ew = b.op("scalar", lambda e: e.activation(ew[:], wdec[:], AF.Exp), waits=[m_neg])
    m_ecw = b.op("scalar", lambda e: e.activation(ecw[:], wdec[:], AF.Exp, scale=float(TCA)))
    ew03 = ew0[:].rearrange("p (i c) -> p i c", i=8)
    ejw3 = ejw[:].rearrange("p (i c) -> p i c", i=8)
    b.op("gpsimd", lambda e: e.memset(onesB[:], 1.0))
    m_iden = b.op("gpsimd", lambda e: e.affine_select(
        idenB[:], onesB[:], [[1, 128]], ALU.is_equal, 0.0, base=0, channel_multiplier=-1))
    m_iota = b.op("gpsimd", lambda e: e.iota(ejw3, [[0, 8], [1, TCA]], base=0,
                                             channel_multiplier=0,
                                             allow_small_or_imprecise_dtypes=True))
    m_jw = b.op("vector", lambda e: e.tensor_tensor(
        ejw3, ejw3, wdec[:].unsqueeze(2).broadcast_to([P, 8, TCA]), ALU.mult),
        waits=[m_iota, m_neg])
    m_ejw = b.op("scalar", lambda e: e.activation(ejw[:], ejw[:], AF.Exp), waits=[m_jw])
    m_ew0a = b.op("vector", lambda e: e.tensor_copy(
        ew03, ew[:].unsqueeze(2).broadcast_to([P, 8, TCA])), waits=[m_ew])
    m_ew0b = b.op("vector", lambda e: e.memset(ew03[:, :, 0:1], 0.0))
    m_rwc = b.op("vector", lambda e: e.tensor_copy(rwT[:], rwTf[:].transpose([0, 2, 1])))
    for t_ in (aa, bb):
        b.op("vector", lambda e, t_=t_: e.memset(t_[:], 0.0))
    b.op("vector", lambda e: e.memset(pst[:], P_INIT))
    b.op("vector", lambda e: e.memset(carry1[:], 0.0))
    m_init = b.op("vector", lambda e: e.memset(carry2[:], 0.0))

    prog = {"xn_rows": 0, "s_rows": 0, "expert_cast": 0, "g_rows": 0,
            "xn_sp": [], "s_sp": [], "g_sp": []}

    def ln_tile(src3, tt, ecol, out_bf, waits0):
        o = 6 * tt
        sqs = flat(ecb)[:, 0:1024]
        b.op("vector", lambda e, o=o, tt=tt: e.tensor_reduce(
            stat[:, o:o + 1].unsqueeze(2), src3[:, tt:tt + 1, :], AX.X, ALU.add),
            waits=waits0)
        b.op("vector", lambda e, tt=tt, sqs=sqs: e.tensor_tensor(
            sqs, src3[:, tt, :], src3[:, tt, :], ALU.mult))
        b.op("vector", lambda e, o=o, sqs=sqs: e.tensor_reduce(
            stat[:, o + 1:o + 2], sqs, AX.X, ALU.add))
        b.op("vector", lambda e, o=o: e.tensor_scalar(
            stat[:, o + 2:o + 3], stat[:, o:o + 1], 1.0 / E, None, ALU.mult))
        b.op("vector", lambda e, o=o: e.tensor_tensor(
            stat[:, o + 4:o + 5], stat[:, o + 2:o + 3], stat[:, o + 2:o + 3], ALU.mult))
        b.op("vector", lambda e, o=o: e.tensor_scalar(
            stat[:, o + 3:o + 4], stat[:, o + 1:o + 2], 1.0 / E, None, ALU.mult))
        m7 = b.op("vector", lambda e, o=o: e.tensor_tensor(
            stat[:, o + 3:o + 4], stat[:, o + 3:o + 4], stat[:, o + 4:o + 5], ALU.subtract))
        b.op("scalar", lambda e, o=o, ecol=ecol: e.activation(
            stat[:, o + 3:o + 4], stat[:, o + 3:o + 4], AF.Ln, bias=epsT[:, ecol:ecol + 1]),
            waits=[m7])
        m9 = b.op("scalar", lambda e, o=o: e.activation(
            stat[:, o + 5:o + 6], stat[:, o + 3:o + 4], AF.Exp, scale=-0.5))
        return b.op("vector", lambda e, o=o, tt=tt, out_bf=out_bf: e.tensor_scalar(
            out_bf[:, tt, :], src3[:, tt, :], stat[:, o + 2:o + 3], stat[:, o + 5:o + 6],
            ALU.subtract, ALU.mult), waits=[m9])

    def transpose_in(src_bf, dst, gvec, bvec, src_ms):
        mh = None
        for tt in range(2):
            bank = AP_[tt % 2]
            psbf = PS[bank][:].bitcast(BF16)
            for i in range(8):
                wts_ = []
                if i == 0:
                    wts_ = [m_iden, src_ms] + \
                           ([psum_free[bank]] if psum_free[bank] else [])
                b.mm(lambda e, i=i, tt=tt, psbf=psbf, src_bf=src_bf: e.transpose(
                    psbf[:, 128 * i:128 * (i + 1)],
                    src_bf[:, tt, 128 * i:128 * (i + 1)], idenB[:]),
                    waits=wts_, mark=(i == 7))
            gm = ("pe", b.c["pe"])
            for i in range(8):
                mh = b.op("vector", lambda e, i=i, tt=tt, psbf=psbf, dst=dst,
                          gvec=gvec, bvec=bvec: e.tensor_scalar(
                    dst[:, i, 128 * tt:128 * (tt + 1)], psbf[:, 128 * i:128 * (i + 1)],
                    gvec[:, i:i + 1], bvec[:, i:i + 1], ALU.mult, ALU.add),
                    waits=[gm] if i == 0 else [])
            psum_free[bank] = mh
        return mh, ("pe", b.c["pe"])

    def gen_A():
        last_x0_use = None
        ld_x0 = None
        last_s_rd = []
        kc_free = None
        t1b_free = None
        wslot_free = [None] * 4
        rslot_free = [None] * 4
        cast_i = 0
        dbg_xn = None
        dbg_wkv = None
        fc_st = None
        fc_m1 = None
        fc_m2 = None
        fc_next = 0

        def emit_fc(cc, base_mark):
            nonlocal fc_st, fc_m1, fc_m2
            r0f = cc * TCA
            g_w = prog["g_sp"][cc // 2]
            fx = flat(kT).rearrange("p (t e) -> p t e", t=2)
            fs = flat(sab).rearrange("p (t e) -> p t e", t=2)
            fgv = flat(vT).rearrange("p (t e) -> p t e", t=2)
            l1 = b.dma("sync", lambda e, r0f=r0f, fx=fx: e.dma_start(
                fx, rearr_rows(I["x"], r0f, TCA)),
                waits=[base_mark, dbg_wkv, fc_st])
            l2 = b.dma("scalar", lambda e, r0f=r0f, fs=fs: e.dma_start(
                fs, rearr_rows(S["s_f"], r0f, TCA)),
                waits=[base_mark, fc_m2, prog["s_sp"][cc]])
            l3 = b.dma("scalar", lambda e, r0f=r0f, fgv=fgv: e.dma_start(
                fgv, rearr_rows(S["g_b"], r0f, TCA)),
                waits=[base_mark, fc_m1, g_w])
            fc_m1 = b.op("vector", lambda e, fs=fs, fgv=fgv: e.tensor_tensor(
                flat(fs), flat(fs), flat(fgv), ALU.mult), waits=[l2, l3])
            fc_m2 = b.op("vector", lambda e, fx=fx, fs=fs: e.tensor_tensor(
                flat(fx), flat(fx), flat(fs), ALU.add), waits=[l1])
            fc_st = b.dma("sync", lambda e, r0f=r0f, fx=fx: e.dma_start(
                rearr_rows(y, r0f, TCA), fx), waits=[fc_m2])

        def row_stream(lhsT_src, w_dram, out_banks4, cast_ms, first_wait,
                       kk_base=0, n_kk=8, stop_kk=None):
            res = []
            nonlocal_rs = rslot_free
            if stop_kk is None:
                stop_kk = n_kk - 1
            for nh in range(2):
                pw = {}
                for m8 in range(2):
                    bank = out_banks4[2 * nh + m8]
                    pw[m8] = [psum_free[bank]] if psum_free[bank] else []
                for kk in range(n_kk):
                    slot = kk % 4
                    ldr = b.dma("sync", lambda e, kk=kk, nh=nh, slot=slot,
                                w_dram=w_dram, kk_base=kk_base: e.dma_start(
                        wrow[slot][:], w_dram.rearrange("(kk p) n -> p kk n", p=P)[
                            :, kk_base + kk, 512 * nh:512 * (nh + 1)]),
                        waits=[cast_ms, nonlocal_rs[slot]])
                    for m8 in range(2):
                        bank = out_banks4[2 * nh + m8]
                        wts_ = []
                        if kk == 0:
                            wts_ = [first_wait] + pw[m8]
                        if m8 == 0:
                            wts_.append(ldr)
                        b.mm(lambda e, kk=kk, m8=m8, nh=nh, bank=bank, slot=slot,
                             lhsT_src=lhsT_src: e.matmul(
                            PS[bank][:], lhsT_src[:, kk, 128 * m8:128 * (m8 + 1)],
                            wrow[slot][:], start=(kk == 0), stop=(kk == stop_kk)),
                            waits=wts_, mark=(m8 == 1))
                        if m8 == 1:
                            nonlocal_rs[slot] = ("pe", b.c["pe"])
                for m8 in range(2):
                    res.append((("pe", b.c["pe"]), out_banks4[2 * nh + m8], m8, nh))
            return res

        for c in range(NCA):
            r0 = c * TCA
            for _ in range(2):
                if cast_i < len(expert_casts):
                    ei, kind = expert_casts[cast_i]
                    src = I["w1"][ei] if kind == "bw1" else I["w2"][ei]
                    dst = S[kind][ei]
                    cast_done[(kind, ei)] = b.dma(
                        "gpsimd", lambda e, s_=src, d=dst: e.dma_start(d, s_))
                    cast_i += 1
                    prog["expert_cast"] = cast_i
            if ld_x0 is None:
                ld_x0 = b.dma("sync", lambda e, r0=r0: e.dma_start(
                    x0c, rearr_rows(I["x"], r0, TCA)), waits=[last_x0_use, dbg_xn])
            yield 2
            xn_ms = []
            for tt in range(2):
                w0 = [ld_x0]
                if tt == 0:
                    w0 += last_s_rd + ([m_init] if c == 0 else [])
                xn_ms.append(ln_tile(x0c, tt, 0, xnc, w0))
            last_x0_use = xn_ms[-1]
            if c + 1 < NCA:
                r1 = (c + 1) * TCA
                ld_x0 = b.dma("sync", lambda e, r1=r1: e.dma_start(
                    x0c, rearr_rows(I["x"], r1, TCA)), waits=[last_x0_use, dbg_xn])
            else:
                ld_x0 = None
            yield 16
            xnb = flat(hsh).rearrange("p (t e) -> p t e", t=2)
            mcast = None
            for tt in range(2):
                mcast = b.op("vector", lambda e, tt=tt, xnb=xnb: e.tensor_copy(
                    xnb[:, tt, :], xnc[:, tt, :]))
            sp_xn = b.dma("sync", lambda e, r0=r0, xnb=xnb: e.dma_start(
                rearr_rows(S["xn_b"], r0, TCA), xnb), waits=[mcast])
            prog["xn_sp"].append(sp_xn)
            if _DEBUG:
                dbg_xn = b.dma("sync", lambda e, r0=r0: e.dma_start(
                    rearr_rows(dbg["d_xn"], r0, TCA), xnc), waits=[xn_ms[-1]])
            prog["xn_rows"] = r0 + TCA
            yield 3
            t1_ms = [ln_tile(xnc, tt, 1, t1b,
                             [t1b_free] if tt == 0 and t1b_free else [])
                     for tt in range(2)]
            yield 14
            _, t1b_free = transpose_in(t1b, hT, g1, b1, t1_ms[-1])
            yield 6
            b.op("vector", lambda e: e.tensor_copy(
                hsh[:, :, 1:TCA], hT[:, :, 0:TCA - 1]), waits=[sp_xn])
            m_s2 = b.op("vector", lambda e: e.tensor_copy(
                hsh[:, :, 0:1], carry1[:].unsqueeze(2)))
            b.op("vector", lambda e: e.tensor_copy(
                carry1[:].unsqueeze(2), hT[:, :, TCA - 1:TCA]))
            yield 3
            evac_ms = {}
            mix_v = {"k": (mk, omk), "v": (mv, omv), "r": (mr, omr)}
            wdram = {"k": S["bWk"], "v": S["bWv"], "r": S["bWr"]}
            wcast = {"k": cast_done["bWk"], "v": cast_done["bWv"], "r": cast_done["bWr"]}
            for pi, pn in enumerate(("k", "v", "r")):
                xm = xm0 if pi % 2 == 0 else xm1
                mvec, ovec = mix_v[pn]
                mmix0 = b.op("vector", lambda e, xm=xm, ovec=ovec: e.tensor_tensor(
                    xm[:], hsh[:], ovec[:].unsqueeze(2).broadcast_to([P, 8, TCA]),
                    ALU.mult), waits=[m_s2, ("pe", b.c["pe"])])
                mmix = None
                for i in range(8):
                    mmix = b.op("vector", lambda e, i=i, xm=xm, mvec=mvec:
                                e.scalar_tensor_tensor(
                        xm[:, i, :], hT[:, i, :], mvec[:, i:i + 1], xm[:, i, :],
                        ALU.mult, ALU.add))
                yield 5
                last_ev = None
                for m8 in range(8):
                    slot = m8 % 4
                    ldw = b.dma("sync", lambda e, pn=pn, m8=m8, slot=slot: e.dma_start(
                        wtile[slot][:], wdram[pn][:, 128 * m8:128 * (m8 + 1)].rearrange(
                            "(kk p) n -> p kk n", p=P)),
                        waits=[wcast[pn], wslot_free[slot]])
                    bank = (AB + AP_)[m8 % 4]
                    wts_ = [ldw]
                    if m8 == 0:
                        wts_.append(mmix)
                    if psum_free[bank]:
                        wts_.append(psum_free[bank])
                    for kk in range(8):
                        b.mm(lambda e, kk=kk, slot=slot, bank=bank, xm=xm: e.matmul(
                            PS[bank][:, 0:TCA], wtile[slot][:, kk, :], xm[:, kk, :],
                            start=(kk == 0), stop=(kk == 7)),
                            waits=wts_ if kk == 0 else [], mark=(kk == 7))
                    gm = ("pe", b.c["pe"])
                    wslot_free[slot] = gm
                    if pn == "k":
                        ev = b.op("vector", lambda e, m8=m8, bank=bank: e.tensor_copy(
                            kT[:, m8, :], PS[bank][:, 0:TCA]),
                            waits=[gm, dbg_wkv, fc_st] if m8 == 0 else [gm])
                    elif pn == "v":
                        ev = b.op("vector", lambda e, m8=m8, bank=bank: e.tensor_copy(
                            vT[:, m8, :], PS[bank][:, 0:TCA]), waits=[gm])
                    else:
                        ev = b.op("scalar", lambda e, m8=m8, bank=bank: e.activation(
                            xm1[:, m8, :], PS[bank][:, 0:TCA], AF.Sigmoid), waits=[gm])
                    psum_free[bank] = ev
                    last_ev = ev
                    if m8 % 2 == 1:
                        yield 2.2
                evac_ms[pn] = last_ev
            dbg_k = None
            if _DEBUG:
                dbg_k = b.dma("sync", lambda e, r0=r0: e.dma_start(
                    dbg["d_k"].rearrange("(i p) t -> p i t", p=P)[:, :, r0:r0 + TCA], kT),
                    waits=[evac_ms["k"]])
            m_ng = b.op("vector", lambda e: e.tensor_reduce(
                ng[:].unsqueeze(2), kT, AX.X, ALU.max, negate=True), waits=[evac_ms["k"]])
            m_ug = b.op("vector", lambda e: e.tensor_tensor(ug[:], uvec[:], ng[:], ALU.add))
            m_ec = None
            for i in range(8):
                m_ec = b.op("scalar", lambda e, i=i: e.activation(
                    ecb[:, i, :], kT[:, i, :], AF.Exp, bias=ng[:, i:i + 1]),
                    waits=([m_ng] if i == 0 else []))
            m_za = b.op("vector", lambda e: e.tensor_tensor(zab, ecb, vT[:], ALU.mult),
                        waits=[m_ec, evac_ms["v"]])
            yield 4
            m_sa = b.op("vector", lambda e: e.tensor_tensor_scan(
                flat(sab), ew0[:], flat(zab), 0.0, ALU.mult, ALU.add),
                waits=[m_ew0b, m_ew0a])
            yield 5
            m_sb = b.op("vector", lambda e: e.tensor_tensor_scan(
                flat(zab), ew0[:], flat(ecb), 0.0, ALU.mult, ALU.add))
            yield 5
            m_t = b.op("vector", lambda e: e.tensor_tensor(sc8[:], pst[:], ng[:], ALU.add))
            m_eP = b.op("scalar", lambda e: e.activation(sc8[:], sc8[:], AF.Exp),
                        waits=[m_t])
            b.op("vector", lambda e: e.tensor_tensor(aag[:], aa[:], sc8[:], ALU.mult),
                 waits=[m_eP])
            b.op("vector", lambda e: e.tensor_tensor(bbg[:], bb[:], sc8[:], ALU.mult))
            b.op("vector", lambda e: e.tensor_tensor(aa[:], ecw[:], aag[:], ALU.mult),
                 waits=[m_ecw])
            b.op("vector", lambda e: e.tensor_tensor(
                aa[:].unsqueeze(2), aa[:].unsqueeze(2), sab[:, :, TCA - 1:TCA], ALU.add))
            b.op("vector", lambda e: e.tensor_tensor(bb[:], ecw[:], bbg[:], ALU.mult))
            b.op("vector", lambda e: e.tensor_tensor(
                bb[:].unsqueeze(2), bb[:].unsqueeze(2), zab[:, :, TCA - 1:TCA], ALU.add))
            b.op("vector", lambda e: e.tensor_scalar(
                pst[:], ng[:], -1.0, None, ALU.mult))
            m_es = None
            for i in range(8):
                m_es = b.op("scalar", lambda e, i=i: e.activation(
                    ecb[:, i, :], kT[:, i, :], AF.Exp, bias=ug[:, i:i + 1]),
                    waits=([m_sb, m_ug] if i == 0 else []))
            yield 8
            b.op("vector", lambda e: e.tensor_tensor(
                kT, ejw3, aag[:].unsqueeze(2).broadcast_to([P, 8, TCA]), ALU.mult),
                waits=[m_es, m_ejw, dbg_k])
            b.op("vector", lambda e: e.tensor_tensor(
                kT[:, :, 1:], kT[:, :, 1:], sab[:, :, 0:TCA - 1], ALU.add))
            yield 5
            b.op("vector", lambda e: e.tensor_tensor(vT[:], ecb, vT[:], ALU.mult))
            b.op("vector", lambda e: e.tensor_tensor(kT, kT, vT[:], ALU.add))
            yield 5
            b.op("vector", lambda e: e.tensor_tensor(
                sab, ejw3, bbg[:].unsqueeze(2).broadcast_to([P, 8, TCA]), ALU.mult))
            b.op("vector", lambda e: e.tensor_tensor(
                sab[:, :, 1:], sab[:, :, 1:], zab[:, :, 0:TCA - 1], ALU.add))
            yield 5
            m_d3 = b.op("vector", lambda e: e.tensor_tensor(sab, sab, ecb, ALU.add))
            b.op("scalar", lambda e: e.activation(flat(sab), flat(sab), AF.Ln),
                 waits=[m_d3])
            m_r2 = b.op("scalar", lambda e: e.activation(flat(sab), flat(sab), AF.Exp,
                                                         scale=-1.0))
            yield 6
            m_wkv = b.op("vector", lambda e: e.tensor_tensor(kT, kT, sab, ALU.mult),
                         waits=[m_r2])
            if _DEBUG:
                dbg_wkv = b.dma("sync", lambda e, r0=r0: e.dma_start(
                    dbg["d_wkv"].rearrange("(i p) t -> p i t", p=P)[:, :, r0:r0 + TCA], kT),
                    waits=[m_wkv])
            m_rk = b.op("vector", lambda e: e.tensor_tensor(xm0[:], kT, xm1[:], ALU.mult),
                        waits=[evac_ms["r"], ("pe", b.c["pe"])])
            last_mrk = m_rk
            yield 16
            groups = row_stream(xm0, S["bWo"], AB + AP_, cast_done["bWo"], m_rk)
            x1_ms = []
            for (gm_, bank, m8, nh) in groups:
                ev = b.op("vector", lambda e, m8=m8, nh=nh, bank=bank: e.tensor_tensor(
                    xnc[:, m8, 512 * nh:512 * (nh + 1)],
                    xnc[:, m8, 512 * nh:512 * (nh + 1)], PS[bank][:], ALU.add),
                    waits=[gm_])
                psum_free[bank] = ev
                x1_ms.append(ev)
            yield 8
            drained = 0
            while (fc_next <= c - 2 and drained < 2
                   and prog["g_rows"] >= fc_next * TCA + TCA):
                emit_fc(fc_next, m_rk)
                fc_next += 1
                drained += 1
                yield 5
            if _DEBUG:
                b.dma("sync", lambda e, r0=r0: e.dma_start(
                    rearr_rows(dbg["d_x1"], r0, TCA), xnc), waits=[x1_ms[-1]])
            t2_ms = [ln_tile(xnc, tt, 1, t1b,
                             [x1_ms[-1], t1b_free] if tt == 0 else [])
                     for tt in range(2)]
            yield 14
            _, t1b_free = transpose_in(t1b, hT, g2, b2, t2_ms[-1])
            yield 6
            b.op("vector", lambda e: e.tensor_copy(hsh[:, :, 1:TCA], hT[:, :, 0:TCA - 1]))
            m2_s2 = b.op("vector", lambda e: e.tensor_copy(
                hsh[:, :, 0:1], carry2[:].unsqueeze(2)))
            b.op("vector", lambda e: e.tensor_copy(
                carry2[:].unsqueeze(2), hT[:, :, TCA - 1:TCA]))
            yield 3
            b.op("vector", lambda e: e.tensor_tensor(
                xm1[:], hsh[:], ocmr[:].unsqueeze(2).broadcast_to([P, 8, TCA]), ALU.mult),
                waits=[m2_s2, ("pe", b.c["pe"])])
            mmxr = None
            for i in range(8):
                mmxr = b.op("vector", lambda e, i=i: e.scalar_tensor_tensor(
                    xm1[:, i, :], hT[:, i, :], cmr[:, i:i + 1], xm1[:, i, :],
                    ALU.mult, ALU.add))
            yield 5
            groups = row_stream(xm1, S["bcWr"], AB + AP_, cast_done["bcWr"], mmxr)
            for (gm_, bank, m8, nh) in groups:
                ev = b.op("scalar", lambda e, m8=m8, nh=nh, bank=bank: e.activation(
                    rc2[:, m8, 512 * nh:512 * (nh + 1)], PS[bank][:], AF.Sigmoid),
                    waits=[gm_])
                psum_free[bank] = ev
            rc_mark = ("pe", b.c["pe"])
            yield 8
            b.op("vector", lambda e: e.tensor_tensor(
                xm1[:], hsh[:], ocmk[:].unsqueeze(2).broadcast_to([P, 8, TCA]), ALU.mult),
                waits=[rc_mark])
            mmxk = None
            for i in range(8):
                mmxk = b.op("vector", lambda e, i=i: e.scalar_tensor_tensor(
                    xm1[:, i, :], hT[:, i, :], cmk[:, i:i + 1], xm1[:, i, :],
                    ALU.mult, ALU.add))
            yield 5
            s_ms = []
            rel_free = [None, None]
            for fh in range(2):
                sq_ms = None
                for fm in range(16):
                    fmg = 16 * fh + fm
                    slot = fm % 4
                    ldt = b.dma("sync", lambda e, fmg=fmg, slot=slot: e.dma_start(
                        wtile[slot][:], S["bcWk"][:, 128 * fmg:128 * (fmg + 1)].rearrange(
                            "(kk p) n -> p kk n", p=P)),
                        waits=[cast_done["bcWk"], wslot_free[slot]])
                    bank = AP_[fm % 2]
                    wts_ = [ldt]
                    if fm == 0:
                        wts_.append(mmxk)
                        if fh == 0 and kc_free:
                            wts_.append(kc_free)
                    if psum_free[bank]:
                        wts_.append(psum_free[bank])
                    for kk in range(8):
                        b.mm(lambda e, kk=kk, slot=slot, bank=bank: e.matmul(
                            PS[bank][:, 0:TCA], wtile[slot][:, kk, :], xm1[:, kk, :],
                            start=(kk == 0), stop=(kk == 7)),
                            waits=wts_ if kk == 0 else [], mark=(kk == 7))
                    gm = ("pe", b.c["pe"])
                    wslot_free[slot] = gm
                    rel = relA if fm % 2 == 0 else relB
                    evr = b.op("scalar", lambda e, bank=bank, rel=rel: e.activation(
                        rel[:], PS[bank][:, 0:TCA], AF.Relu),
                        waits=[gm, rel_free[fm % 2]])
                    psum_free[bank] = evr
                    sq_ms = b.op("vector", lambda e, fm=fm, rel=rel: e.tensor_tensor(
                        kcH[:, fm, :], rel[:], rel[:], ALU.mult), waits=[evr])
                    rel_free[fm % 2] = sq_ms
                    if fm % 2 == 1:
                        yield 2.2
                groups = row_stream(kcH, S["bcWv"], AB + AP_, cast_done["bcWv"], sq_ms,
                                    kk_base=16 * fh, n_kk=16, stop_kk=15)
                for (gm_, bank, m8, nh) in groups:
                    mt = b.op("vector", lambda e, m8=m8, nh=nh, bank=bank: e.tensor_tensor(
                        tmp5[:], PS[bank][:], rc2[:, m8, 512 * nh:512 * (nh + 1)],
                        ALU.mult), waits=[gm_])
                    ma = b.op("vector", lambda e, m8=m8, nh=nh: e.tensor_tensor(
                        xnc[:, m8, 512 * nh:512 * (nh + 1)],
                        xnc[:, m8, 512 * nh:512 * (nh + 1)], tmp5[:], ALU.add))
                    psum_free[bank] = ma
                    s_ms.append(ma)
                kc_free = ("pe", b.c["pe"])
                yield 14
            sp_s = b.dma("sync", lambda e, r0=r0: e.dma_start(
                rearr_rows(S["s_f"], r0, TCA), xnc), waits=[s_ms[-1]])
            last_s_rd = [sp_s]
            if _DEBUG:
                last_s_rd.append(b.dma("sync", lambda e, r0=r0: e.dma_start(
                    rearr_rows(dbg["d_s"], r0, TCA), xnc), waits=[s_ms[-1]]))
            prog["s_sp"].append(sp_s)
            prog["s_rows"] = r0 + TCA
            yield 3
        while fc_next < NCA:
            while prog["g_rows"] < fc_next * TCA + TCA:
                yield ("stall",)
            emit_fc(fc_next, last_mrk)
            fc_next += 1
            yield 5

    def gen_F():
        last_xnT = None
        last_gout = None
        last_gacc = None
        w2_free = [None, None]
        w1_free = [None, None, None, None]

        pre_lds = None
        for cf in range(NCF):
            r0 = cf * TCF
            if pre_lds is not None:
                xnT_lds = pre_lds
                pre_lds = None
            else:
                while prog["xn_rows"] < r0 + TCF:
                    yield ("stall",)
                xn_sp_w = [prog["xn_sp"][2 * cf], prog["xn_sp"][2 * cf + 1]]
                xnT_lds = []
                for i in range(8):
                    xnT_lds.append(b.dma("scalar", lambda e, r0=r0, i=i: e.dma_start(
                        xnT[:, i, :], S["xn_b"][r0:r0 + TCF, 128 * i:128 * (i + 1)],
                        transpose=True),
                        waits=([last_xnT] + xn_sp_w) if i == 0 else []))
                yield 8
            lg = None
            for m8 in range(4):
                bank = FC_[0]
                wts_ = (xnT_lds + [m_rwc]) if m8 == 0 else []
                if psum_free[bank]:
                    wts_ = wts_ + [psum_free[bank]]
                for kk in range(8):
                    b.mm(lambda e, m8=m8, kk=kk, bank=bank: e.matmul(
                        PS[bank][:, 0:NE], xnT[:, kk, 128 * m8:128 * (m8 + 1)],
                        rwT[:, kk, :], start=(kk == 0), stop=(kk == 7)),
                        waits=wts_, mark=(kk == 7))
                gm = ("pe", b.c["pe"])
                lg = b.op("vector", lambda e, m8=m8, bank=bank: e.tensor_copy(
                    logit[:, m8, :], PS[bank][:, 0:NE]), waits=[gm])
                psum_free[bank] = lg
            yield 4
            b.op("vector", lambda e: e.tensor_reduce(
                m1n[:].unsqueeze(2), logit[:], AX.X, ALU.max, negate=True), waits=[lg])
            b.op("vector", lambda e: e.tensor_tensor(
                tkA[:], logit[:], m1n[:].unsqueeze(2).broadcast_to([P, 4, NE]), ALU.add))
            b.op("vector", lambda e: e.tensor_scalar(
                tkB[:], tkA[:], 0.0, None, ALU.is_equal))
            b.op("vector", lambda e: e.scalar_tensor_tensor(
                tkA[:], tkB[:], -1e30, logit[:], ALU.mult, ALU.add))
            b.op("vector", lambda e: e.tensor_reduce(
                m2n[:].unsqueeze(2), tkA[:], AX.X, ALU.max, negate=True))
            b.op("vector", lambda e: e.tensor_tensor(
                tkA[:], tkA[:], m2n[:].unsqueeze(2).broadcast_to([P, 4, NE]), ALU.add))
            b.op("vector", lambda e: e.tensor_scalar(
                tkA[:], tkA[:], 0.0, None, ALU.is_equal))
            mH = b.op("vector", lambda e: e.tensor_tensor(tkB[:], tkB[:], tkA[:], ALU.add))
            mI = None
            for m8 in range(4):
                mI = b.op("scalar", lambda e, m8=m8: e.activation(
                    wts[:, m8, :], logit[:, m8, :], AF.Exp, bias=m1n[:, m8:m8 + 1]),
                    waits=([mH] if m8 == 0 else []))
            b.op("vector", lambda e: e.tensor_tensor(wts[:], wts[:], tkB[:], ALU.mult),
                 waits=[mI])
            b.op("vector", lambda e: e.tensor_reduce(
                zsum[:].unsqueeze(2), wts[:], AX.X, ALU.add))
            b.op("vector", lambda e: e.reciprocal(zsum[:], zsum[:]))
            mW = b.op("vector", lambda e: e.tensor_tensor(
                wts[:], wts[:], zsum[:].unsqueeze(2).broadcast_to([P, 4, NE]), ALU.mult))
            mz = b.op("vector", lambda e: e.memset(flat(gacc), 0.0), waits=[last_gacc])
            yield 6
            for ei in range(NE):
                while prog["expert_cast"] < 2 * (ei + 1):
                    yield ("stall",)
                for h in range(2):
                    slot2 = h
                    w2ld = b.dma("scalar", lambda e, ei=ei, h=h, slot2=slot2: e.dma_start(
                        w2s[slot2][:], S["bw2"][ei][1024 * h:1024 * (h + 1), :].rearrange(
                            "(kk p) n -> p kk n", p=P)),
                        waits=[cast_done[("bw2", ei)], w2_free[slot2]])
                    he_last = None
                    w1_lds = {}

                    def emit_w1(m8, ei=ei, h=h, w1_lds=w1_lds):
                        slot = m8 % 4
                        col = 1024 * h + 128 * m8
                        w1_lds[m8] = b.dma("scalar", lambda e, ei=ei, col=col, slot=slot:
                                           e.dma_start(
                            w1t[slot][:], S["bw1"][ei][:, col:col + 128].rearrange(
                                "(kk p) n -> p kk n", p=P)),
                            waits=[cast_done[("bw1", ei)], w1_free[slot]])

                    emit_w1(0)
                    emit_w1(1)
                    for m8 in range(8):
                        slot = m8 % 4
                        bank = FB[m8 % 2]
                        wts_ = [w1_lds[m8]]
                        if psum_free[bank]:
                            wts_.append(psum_free[bank])
                        for kk in range(8):
                            b.mm(lambda e, kk=kk, slot=slot, bank=bank: e.matmul(
                                PS[bank][:], w1t[slot][:, kk, :], xnT[:, kk, :],
                                start=(kk == 0), stop=(kk == 7)),
                                waits=wts_ if kk == 0 else [], mark=(kk == 7))
                            if kk == 0 and m8 + 2 < 8:
                                emit_w1(m8 + 2)
                        gm = ("pe", b.c["pe"])
                        w1_free[slot] = gm
                        he_last = b.op("scalar", lambda e, m8=m8, bank=bank: e.activation(
                            heT[:, m8, :], PS[bank][:], AF.Gelu), waits=[gm])
                        psum_free[bank] = he_last
                        yield 1.9
                    for m8 in range(4):
                        for nh in range(2):
                            bank = FC_[nh % 2]
                            wts_ = []
                            if m8 == 0 and nh == 0:
                                wts_ = [he_last, w2ld]
                            if psum_free[bank]:
                                wts_ = wts_ + [psum_free[bank]]
                            for kk2 in range(8):
                                b.mm(lambda e, kk2=kk2, m8=m8, nh=nh, slot2=slot2,
                                     bank=bank: e.matmul(
                                    PS[bank][:], heT[:, kk2, 128 * m8:128 * (m8 + 1)],
                                    w2s[slot2][:, kk2, 512 * nh:512 * (nh + 1)],
                                    start=(kk2 == 0), stop=(kk2 == 7)),
                                    waits=wts_, mark=(kk2 == 7))
                            gm = ("pe", b.c["pe"])
                            ev = b.op("vector", lambda e, m8=m8, nh=nh, ei=ei, bank=bank:
                                      e.scalar_tensor_tensor(
                                gacc[:, m8, 512 * nh:512 * (nh + 1)], PS[bank][:],
                                wts[:, m8, ei:ei + 1],
                                gacc[:, m8, 512 * nh:512 * (nh + 1)],
                                ALU.mult, ALU.add),
                                waits=[gm] + ([mW, mz] if m8 == 0 and nh == 0 and h == 0
                                              else []))
                            psum_free[bank] = ev
                            yield 1.9
                    w2_free[slot2] = ("pe", b.c["pe"])
            last_xnT = ("pe", b.c["pe"])
            if cf + 1 < NCF and prog["xn_rows"] >= (cf + 1) * TCF + TCF:
                r1 = (cf + 1) * TCF
                w_sp = [prog["xn_sp"][2 * (cf + 1)], prog["xn_sp"][2 * (cf + 1) + 1]]
                pre_lds = []
                for i in range(8):
                    pre_lds.append(b.dma("scalar", lambda e, r1=r1, i=i: e.dma_start(
                        xnT[:, i, :], S["xn_b"][r1:r1 + TCF, 128 * i:128 * (i + 1)],
                        transpose=True),
                        waits=([last_xnT] + w_sp) if i == 0 else []))
            gdone = ("v", b.c["v"])
            m_g = b.op("scalar", lambda e: e.activation(flat(gout), flat(gacc), AF.Silu),
                       waits=[gdone, last_gout])
            last_gacc = m_g
            sp_g = b.dma("scalar", lambda e, r0=r0: e.dma_start(
                rearr_rows(S["g_b"], r0, TCF), gout), waits=[m_g])
            last_gout = sp_g
            dbg_g = None
            if _DEBUG:
                dbf = b.op("vector", lambda e: e.tensor_copy(flat(gacc), flat(gout)),
                           waits=[m_g])
                dbg_g = b.dma("sync", lambda e, r0=r0: e.dma_start(
                    rearr_rows(dbg["d_g"], r0, TCF), gacc), waits=[dbf])
                last_gacc = dbg_g
            yield 4
            prog["g_sp"].append(sp_g)
            prog["g_rows"] = r0 + TCF
            last_gacc = m_g if not _DEBUG else last_gacc

    ag = gen_A()
    fg = gen_F()
    tA = 0.0
    tF = 0.0
    a_alive = f_alive = True
    a_stalled = f_stalled = False
    while a_alive or f_alive:
        if a_stalled and f_stalled:
            raise RuntimeError("driver deadlock: both streams stalled")
        if not f_alive:
            pull_a = True
        elif not a_alive:
            pull_a = False
        elif f_stalled:
            pull_a = True
        elif a_stalled:
            pull_a = False
        else:
            pull_a = tA + 60.0 <= tF
        if pull_a:
            try:
                r = next(ag)
                if isinstance(r, tuple):
                    a_stalled = True
                else:
                    tA += r
                    f_stalled = False
            except StopIteration:
                a_alive = False
                tA = float("inf")
                f_stalled = False
        else:
            try:
                r = next(fg)
                if isinstance(r, tuple):
                    f_stalled = True
                else:
                    tF += r
                    a_stalled = False
            except StopIteration:
                f_alive = False
                tF = float("inf")
                a_stalled = False

    fin = b.queue_done_waits("sync") + b.queue_done_waits("scalar")

    def _final(e, fin=fin):
        for (k, v) in fin:
            e.wait_ge(sems[k], v)
    b.raw("sync", _final)

    with nc.Block() as block:
        block.sync(lambda e: [f(e) for f in b.q["sync"]])
        block.gpsimd(lambda e: [f(e) for f in b.q["gpsimd"]])
        block.scalar(lambda e: [f(e) for f in b.q["scalar"]])
        block.vector(lambda e: [f(e) for f in b.q["vector"]])
        block.tensor(lambda e: [f(e) for f in b.q["tensor"]])

    for cm in reversed(sem_cms):
        cm.__exit__(None, None, None)
    return nc


_NC_CACHE = None


def kernel(**inputs):
    global _NC_CACHE
    if _NC_CACHE is None:
        _NC_CACHE = build_nc()
    nc = _NC_CACHE

    x = np.ascontiguousarray(np.asarray(inputs["L0_attn_inputs"], np.float32))
    base = {
        "ln1g": np.asarray(inputs["L0_attn_ln1_g"], np.float32),
        "ln1b": np.asarray(inputs["L0_attn_ln1_b"], np.float32),
        "ln2g": np.asarray(inputs["L0_attn_ln2_g"], np.float32),
        "ln2b": np.asarray(inputs["L0_attn_ln2_b"], np.float32),
        "mixk": np.asarray(inputs["L0_attn_tm_mix_k"], np.float32),
        "mixv": np.asarray(inputs["L0_attn_tm_mix_v"], np.float32),
        "mixr": np.asarray(inputs["L0_attn_tm_mix_r"], np.float32),
        "cmixk": np.asarray(inputs["L0_attn_cm_mix_k"], np.float32),
        "cmixr": np.asarray(inputs["L0_attn_cm_mix_r"], np.float32),
        "td": np.asarray(inputs["L0_attn_time_decay"], np.float32),
        "tf": np.asarray(inputs["L0_attn_time_first"], np.float32),
        "Wk": np.asarray(inputs["L0_attn_tm_Wk"], np.float32),
        "Wv": np.asarray(inputs["L0_attn_tm_Wv"], np.float32),
        "Wr": np.asarray(inputs["L0_attn_tm_Wr"], np.float32),
        "Wo": np.asarray(inputs["L0_attn_Wo"], np.float32),
        "cWk": np.asarray(inputs["L0_attn_cm_Wk"], np.float32),
        "cWv": np.asarray(inputs["L0_attn_cm_Wv"], np.float32),
        "cWr": np.asarray(inputs["L0_attn_cm_Wr"], np.float32),
        "rw": np.asarray(inputs["L0_ffn_router_w"], np.float32),
        "w1": np.asarray(inputs["L0_ffn_w1"], np.float32),
        "w2": np.asarray(inputs["L0_ffn_w2"], np.float32),
    }
    in_maps = [dict(base, x=x[bi]) for bi in range(8)]
    res = run_bass_kernel_spmd(nc, in_maps, core_ids=list(range(8)))
    out = np.stack([res.results[bi]["y"] for bi in range(8)], axis=0).astype(np.float32)
    if _DEBUG:
        kernel.debug_results = res.results
    return out


# version 09
# speedup vs baseline: 1.0385x; 1.0385x over previous
import numpy as np

import concourse.bass as bass
import concourse.mybir as mybir
from concourse.bass_utils import run_bass_kernel_spmd

F32 = mybir.dt.float32
BF16 = mybir.dt.bfloat16
AF = mybir.ActivationFunctionType
ALU = mybir.AluOpType
AX = mybir.AxisListType

P = 128
T = 4096
E = 1024
H = 1024
F4 = 4096
FE = 2048
NE = 8

TCA = 256
NCA = T // TCA
TCF = 512
NCF = T // TCF

LNEPS_OUT = 1e-5
LNEPS_IN = 1e-3
P_INIT = -1e30

_DEBUG = False


class Bld:
    ENGS = ("sync", "gpsimd", "scalar", "vector", "tensor")
    SELF = {"vector": "v", "scalar": "a", "gpsimd": "p"}
    NQSEM = 8

    def __init__(self, nc, sems):
        self.nc = nc
        self.sems = sems
        self.q = {e: [] for e in self.ENGS}
        self.c = {k: 0 for k in ("v", "a", "p", "pe")}
        for base in ("sq", "wq", "gq"):
            self.c[base] = 0
            for i in range(self.NQSEM):
                self.c[f"{base}{i}"] = 0

    def _mk_waits(self, waits):
        out = []
        for w in waits:
            if w is None:
                continue
            (s, val) = w
            if val > 0:
                out.append((self.sems[s], val))
        return out

    def op(self, eng, fn, waits=()):
        sc = self.SELF.get(eng)
        w = self._mk_waits(waits)
        if sc is not None:
            k = self.c[sc]
            sem = self.sems[sc]

            def run(e, fn=fn, w=w, k=k, sem=sem):
                for (s, val) in w:
                    e.wait_ge(s, val)
                if k > 0:
                    e.wait_ge(sem, k)
                fn(e).then_inc(sem, 1)

            self.q[eng].append(run)
            self.c[sc] = k + 1
            return (sc, k + 1)
        else:
            def run(e, fn=fn, w=w):
                for (s, val) in w:
                    e.wait_ge(s, val)
                fn(e)

            self.q[eng].append(run)
            return None

    def mm(self, fn, waits=(), mark=False):
        w = self._mk_waits(waits)
        if mark:
            self.c["pe"] += 1
            pe_sem = self.sems["pe"]

            def run(e, fn=fn, w=w, pe_sem=pe_sem):
                for (s, val) in w:
                    e.wait_ge(s, val)
                fn(e).then_inc(pe_sem, 1)

            self.q["tensor"].append(run)
            return ("pe", self.c["pe"])
        else:
            def run(e, fn=fn, w=w):
                for (s, val) in w:
                    e.wait_ge(s, val)
                fn(e)

            self.q["tensor"].append(run)
            return None

    def dma(self, queue, fn, waits=()):
        base = {"sync": "sq", "scalar": "wq", "gpsimd": "gq"}[queue]
        idx = self.c[base] % self.NQSEM
        key = f"{base}{idx}"
        sem = self.sems[key]
        prev_done = self.c[key]
        self.c[base] += 1
        self.c[key] = prev_done + 16
        w = self._mk_waits(waits)

        def run(e, fn=fn, w=w, sem=sem, prev_done=prev_done):
            if prev_done > 0:
                e.wait_ge(sem, prev_done)
            for (s, val) in w:
                e.wait_ge(s, val)
            fn(e).then_inc(sem, 16)

        self.q[queue].append(run)
        return (key, self.c[key])

    def queue_done_waits(self, queue):
        base = {"sync": "sq", "scalar": "wq", "gpsimd": "gq"}[queue]
        return [(f"{base}{i}", self.c[f"{base}{i}"]) for i in range(self.NQSEM)
                if self.c.get(f"{base}{i}", 0) > 0]

    def raw(self, eng, fn):
        self.q[eng].append(fn)


def rearr_rows(dram_ap, r0, rows):
    return dram_ap[r0:r0 + rows, :].rearrange("(t p) e -> p t e", p=P)


def build_nc():
    nc = bass.Bass("TRN2", target_bir_lowering=False, debug=False)

    I = {}
    I["x"] = nc.dram_tensor("x", [T, E], F32, kind="ExternalInput").ap()
    for nm, shp in [("ln1g", [E]), ("ln1b", [E]), ("ln2g", [E]), ("ln2b", [E]),
                    ("mixk", [E]), ("mixv", [E]), ("mixr", [E]),
                    ("cmixk", [E]), ("cmixr", [E]), ("td", [H]), ("tf", [H])]:
        I[nm] = nc.dram_tensor(nm, shp, F32, kind="ExternalInput").ap()
    for nm, shp in [("Wk", [E, H]), ("Wv", [E, H]), ("Wr", [E, H]), ("Wo", [H, E]),
                    ("cWk", [E, F4]), ("cWv", [F4, E]), ("cWr", [E, E]),
                    ("rw", [NE, E]), ("w1", [NE, E, FE]), ("w2", [NE, FE, E])]:
        I[nm] = nc.dram_tensor(nm, shp, F32, kind="ExternalInput").ap()

    y = nc.dram_tensor("y", [T, E], F32, kind="ExternalOutput").ap()

    S = {}
    for nm, shp in [("bWk", [E, H]), ("bWv", [E, H]), ("bWr", [E, H]), ("bWo", [H, E]),
                    ("bcWk", [E, F4]), ("bcWv", [F4, E]), ("bcWr", [E, E]),
                    ("bw1", [NE, E, FE]), ("bw2", [NE, FE, E]),
                    ("xn_b", [T, E]), ("g_b", [T, E])]:
        S[nm] = nc.dram_tensor(nm, shp, BF16, kind="Internal").ap()
    S["s_f"] = nc.dram_tensor("s_f", [T, E], F32, kind="Internal").ap()

    dbg = {}
    if _DEBUG:
        for nm, shp in [("d_xn", [T, E]), ("d_k", [H, T]), ("d_wkv", [H, T]),
                        ("d_x1", [T, E]), ("d_s", [T, E]), ("d_g", [T, E])]:
            dbg[nm] = nc.dram_tensor(nm, shp, F32, kind="ExternalOutput").ap()

    def sb(name, shape, dt):
        return nc.alloc_sbuf_tensor(name, shape, dt)

    g1 = sb("g1", [P, 8], F32); b1 = sb("b1", [P, 8], F32)
    g2 = sb("g2", [P, 8], F32); b2 = sb("b2", [P, 8], F32)
    mk = sb("mk", [P, 8], F32); mv = sb("mv", [P, 8], F32); mr = sb("mr", [P, 8], F32)
    omk = sb("omk", [P, 8], F32); omv = sb("omv", [P, 8], F32); omr = sb("omr", [P, 8], F32)
    cmk = sb("cmk", [P, 8], F32); cmr = sb("cmr", [P, 8], F32)
    ocmk = sb("ocmk", [P, 8], F32); ocmr = sb("ocmr", [P, 8], F32)
    uvec = sb("uvec", [P, 8], F32)
    wdec = sb("wdec", [P, 8], F32)
    ew = sb("ew", [P, 8], F32)
    ecw = sb("ecw", [P, 8], F32)
    ew0 = sb("ew0", [P, 8 * TCA], F32)
    ejw = sb("ejw", [P, 8 * TCA], F32)
    aa = sb("aa", [P, 8], F32); bb = sb("bb", [P, 8], F32); pst = sb("pst", [P, 8], F32)
    aag = sb("aag", [P, 8], F32); bbg = sb("bbg", [P, 8], F32)
    ng = sb("ng", [P, 8], F32); ug = sb("ug", [P, 8], F32)
    sc8 = sb("sc8", [P, 8], F32)
    carry1 = sb("carry1", [P, 8], BF16)
    carry2 = sb("carry2", [P, 8], BF16)
    stat = sb("stat", [P, 12], F32)
    epsT = sb("epsT", [P, 2], F32)
    idenB = sb("idenB", [P, 128], BF16)
    onesB = sb("onesB", [P, 128], BF16)
    rwT = sb("rwT", [P, 8, NE], BF16)
    rwTf = sb("rwTf", [P, 8, NE], F32)
    logit = sb("logit", [P, 4, NE], F32)
    wts = sb("wts", [P, 4, NE], F32)
    tkA = sb("tkA", [P, 4, NE], F32)
    tkB = sb("tkB", [P, 4, NE], F32)
    m1n = sb("m1n", [P, 4], F32)
    m2n = sb("m2n", [P, 4], F32)
    zsum = sb("zsum", [P, 4], F32)

    x0c = sb("x0c", [P, 2, E], F32)[:]
    xnc = sb("xnc", [P, 2, E], F32)[:]
    kT = sb("kT", [P, 8, TCA], F32)[:]
    ecb = sb("ecb", [P, 8, TCA], F32)[:]
    zab = sb("zab", [P, 8, TCA], F32)[:]
    sab = sb("sab", [P, 8, TCA], F32)[:]
    relA = sb("relA", [P, TCA], F32)[:]
    relB = sb("relB", [P, TCA], F32)[:]
    tmp5 = sb("tmp5", [P, 512], F32)[:]
    t1b = sb("t1b", [P, 2, E], BF16)[:]
    hT = sb("hT", [P, 8, TCA], BF16)[:]
    hsh = sb("hsh", [P, 8, TCA], BF16)[:]
    xm0 = sb("xm0", [P, 8, TCA], BF16)[:]
    xm1 = sb("xm1", [P, 8, TCA], BF16)[:]
    vT = sb("vT", [P, 8, TCA], BF16)[:]
    rc2 = sb("rc2", [P, 2, E], BF16)[:]
    kcH = sb("kcH", [P, 16, TCA], BF16)[:]
    wtile = [sb(f"wtile{i}", [P, 8, 128], BF16) for i in range(4)]
    wrow = [sb(f"wrow{i}", [P, 512], BF16) for i in range(4)]

    gacc = sb("gacc", [P, 4, E], F32)[:]
    xnT = sb("xnT", [P, 8, TCF], BF16)[:]
    heT = sb("heT", [P, 8, TCF], BF16)[:]
    w1t = [sb(f"w1t{i}", [P, 8, 128], BF16) for i in range(4)]
    w2s = [sb(f"w2s{i}", [P, 8, H], BF16) for i in range(2)]
    gout = sb("gout", [P, 4, E], BF16)[:]

    PS = [nc.alloc_psum_tensor(f"ps{i}", [P, 512], F32) for i in range(8)]
    FB = [0, 1]
    FC_ = [2, 3]
    AB = [4, 5]
    AP_ = [6, 7]

    sems = {}
    sem_cms = []
    sem_names = ["v", "a", "p", "pe"]
    for base in ("sq", "wq", "gq"):
        for i in range(Bld.NQSEM):
            sem_names.append(f"{base}{i}")
    for k in sem_names:
        cm = nc.semaphore(f"sem_{k}")
        sems[k] = cm.__enter__()
        sem_cms.append(cm)

    b = Bld(nc, sems)
    psum_free = {i: None for i in range(8)}
    flat = lambda ap3: ap3.rearrange("p a b -> p (a b)")

    cast_done = {}
    for src, dst in [("Wk", "bWk"), ("Wv", "bWv"), ("Wr", "bWr"), ("Wo", "bWo"),
                     ("cWr", "bcWr"), ("cWk", "bcWk"), ("cWv", "bcWv")]:
        cast_done[dst] = b.dma("gpsimd", lambda e, s_=I[src], d=S[dst]: e.dma_start(d, s_))
    expert_casts = [(ei, kind) for ei in range(NE) for kind in ("bw1", "bw2")]

    def vec_load(dst, src):
        return b.dma("sync", lambda e, d=dst, s_=src: e.dma_start(
            d[:], s_.rearrange("(i p) -> p i", p=P), allow_slow_non_contiguous=True))

    pv = []
    for dst, src in [(g1, "ln1g"), (b1, "ln1b"), (g2, "ln2g"), (b2, "ln2b"),
                     (mk, "mixk"), (mv, "mixv"), (mr, "mixr"),
                     (cmk, "cmixk"), (cmr, "cmixr"), (uvec, "tf"), (sc8, "td")]:
        pv.append(vec_load(dst, I[src]))
    rw_load = b.dma("sync", lambda e: e.dma_start(
        rwTf[:], I["rw"].rearrange("e (i p) -> p e i", p=P),
        allow_slow_non_contiguous=True))

    b.op("vector", lambda e: e.memset(epsT[:, 0:1], LNEPS_OUT), waits=pv + [rw_load])
    b.op("vector", lambda e: e.memset(epsT[:, 1:2], LNEPS_IN))
    for mvv, om in ((mk, omk), (mv, omv), (mr, omr), (cmk, ocmk), (cmr, ocmr)):
        b.op("vector", lambda e, mvv=mvv, om=om: e.tensor_scalar(
            om[:], mvv[:], -1.0, 1.0, ALU.mult, ALU.add))
    m_exp_td = b.op("scalar", lambda e: e.activation(wdec[:], sc8[:], AF.Exp), waits=pv)
    m_neg = b.op("vector", lambda e: e.tensor_scalar(wdec[:], wdec[:], -1.0, None, ALU.mult),
                 waits=[m_exp_td])
    m_ew = b.op("scalar", lambda e: e.activation(ew[:], wdec[:], AF.Exp), waits=[m_neg])
    m_ecw = b.op("scalar", lambda e: e.activation(ecw[:], wdec[:], AF.Exp, scale=float(TCA)))
    ew03 = ew0[:].rearrange("p (i c) -> p i c", i=8)
    ejw3 = ejw[:].rearrange("p (i c) -> p i c", i=8)
    b.op("gpsimd", lambda e: e.memset(onesB[:], 1.0))
    m_iden = b.op("gpsimd", lambda e: e.affine_select(
        idenB[:], onesB[:], [[1, 128]], ALU.is_equal, 0.0, base=0, channel_multiplier=-1))
    m_iota = b.op("gpsimd", lambda e: e.iota(ejw3, [[0, 8], [1, TCA]], base=0,
                                             channel_multiplier=0,
                                             allow_small_or_imprecise_dtypes=True))
    m_jw = b.op("vector", lambda e: e.tensor_tensor(
        ejw3, ejw3, wdec[:].unsqueeze(2).broadcast_to([P, 8, TCA]), ALU.mult),
        waits=[m_iota, m_neg])
    m_ejw = b.op("scalar", lambda e: e.activation(ejw[:], ejw[:], AF.Exp), waits=[m_jw])
    m_ew0a = b.op("vector", lambda e: e.tensor_copy(
        ew03, ew[:].unsqueeze(2).broadcast_to([P, 8, TCA])), waits=[m_ew])
    m_ew0b = b.op("vector", lambda e: e.memset(ew03[:, :, 0:1], 0.0))
    m_rwc = b.op("vector", lambda e: e.tensor_copy(rwT[:], rwTf[:].transpose([0, 2, 1])))
    for t_ in (aa, bb):
        b.op("vector", lambda e, t_=t_: e.memset(t_[:], 0.0))
    b.op("vector", lambda e: e.memset(pst[:], P_INIT))
    b.op("vector", lambda e: e.memset(carry1[:], 0.0))
    m_init = b.op("vector", lambda e: e.memset(carry2[:], 0.0))

    prog = {"xn_rows": 0, "s_rows": 0, "expert_cast": 0, "g_rows": 0,
            "xn_sp": [], "s_sp": [], "g_sp": []}

    def ln_tile(src3, tt, ecol, out_bf, waits0):
        o = 6 * tt
        sqs = flat(ecb)[:, 0:1024]
        b.op("vector", lambda e, o=o, tt=tt: e.tensor_reduce(
            stat[:, o:o + 1].unsqueeze(2), src3[:, tt:tt + 1, :], AX.X, ALU.add),
            waits=waits0)
        b.op("vector", lambda e, tt=tt, sqs=sqs: e.tensor_tensor(
            sqs, src3[:, tt, :], src3[:, tt, :], ALU.mult))
        b.op("vector", lambda e, o=o, sqs=sqs: e.tensor_reduce(
            stat[:, o + 1:o + 2], sqs, AX.X, ALU.add))
        b.op("vector", lambda e, o=o: e.tensor_scalar(
            stat[:, o + 2:o + 3], stat[:, o:o + 1], 1.0 / E, None, ALU.mult))
        b.op("vector", lambda e, o=o: e.tensor_tensor(
            stat[:, o + 4:o + 5], stat[:, o + 2:o + 3], stat[:, o + 2:o + 3], ALU.mult))
        b.op("vector", lambda e, o=o: e.tensor_scalar(
            stat[:, o + 3:o + 4], stat[:, o + 1:o + 2], 1.0 / E, None, ALU.mult))
        m7 = b.op("vector", lambda e, o=o: e.tensor_tensor(
            stat[:, o + 3:o + 4], stat[:, o + 3:o + 4], stat[:, o + 4:o + 5], ALU.subtract))
        b.op("scalar", lambda e, o=o, ecol=ecol: e.activation(
            stat[:, o + 3:o + 4], stat[:, o + 3:o + 4], AF.Ln, bias=epsT[:, ecol:ecol + 1]),
            waits=[m7])
        m9 = b.op("scalar", lambda e, o=o: e.activation(
            stat[:, o + 5:o + 6], stat[:, o + 3:o + 4], AF.Exp, scale=-0.5))
        return b.op("vector", lambda e, o=o, tt=tt, out_bf=out_bf: e.tensor_scalar(
            out_bf[:, tt, :], src3[:, tt, :], stat[:, o + 2:o + 3], stat[:, o + 5:o + 6],
            ALU.subtract, ALU.mult), waits=[m9])

    def transpose_in(src_bf, dst, gvec, bvec, src_ms):
        mh = None
        for tt in range(2):
            bank = AP_[tt % 2]
            psbf = PS[bank][:].bitcast(BF16)
            for i in range(8):
                wts_ = []
                if i == 0:
                    wts_ = [m_iden, src_ms] + \
                           ([psum_free[bank]] if psum_free[bank] else [])
                b.mm(lambda e, i=i, tt=tt, psbf=psbf, src_bf=src_bf: e.transpose(
                    psbf[:, 128 * i:128 * (i + 1)],
                    src_bf[:, tt, 128 * i:128 * (i + 1)], idenB[:]),
                    waits=wts_, mark=(i == 7))
            gm = ("pe", b.c["pe"])
            for i in range(8):
                mh = b.op("vector", lambda e, i=i, tt=tt, psbf=psbf, dst=dst,
                          gvec=gvec, bvec=bvec: e.tensor_scalar(
                    dst[:, i, 128 * tt:128 * (tt + 1)], psbf[:, 128 * i:128 * (i + 1)],
                    gvec[:, i:i + 1], bvec[:, i:i + 1], ALU.mult, ALU.add),
                    waits=[gm] if i == 0 else [])
            psum_free[bank] = mh
        return mh, ("pe", b.c["pe"])

    def gen_A():
        last_x0_use = None
        ld_x0 = None
        last_s_rd = []
        kc_free = None
        t1b_free = None
        wslot_free = [None] * 4
        rslot_free = [None] * 4
        cast_i = 0
        dbg_xn = None
        dbg_wkv = None
        fc_st = None
        fc_m1 = None
        fc_m2 = None
        fc_next = 0

        def emit_fc(cc, base_mark):
            nonlocal fc_st, fc_m1, fc_m2
            r0f = cc * TCA
            g_w = prog["g_sp"][cc // 2]
            fx = flat(kT).rearrange("p (t e) -> p t e", t=2)
            fs = flat(sab).rearrange("p (t e) -> p t e", t=2)
            fgv = flat(vT).rearrange("p (t e) -> p t e", t=2)
            l1 = b.dma("gpsimd", lambda e, r0f=r0f, fx=fx: e.dma_start(
                fx, rearr_rows(I["x"], r0f, TCA)),
                waits=[base_mark, dbg_wkv, fc_st])
            l2 = b.dma("gpsimd", lambda e, r0f=r0f, fs=fs: e.dma_start(
                fs, rearr_rows(S["s_f"], r0f, TCA)),
                waits=[base_mark, fc_m2, prog["s_sp"][cc]])
            l3 = b.dma("gpsimd", lambda e, r0f=r0f, fgv=fgv: e.dma_start(
                fgv, rearr_rows(S["g_b"], r0f, TCA)),
                waits=[base_mark, fc_m1, g_w])
            fc_m1 = b.op("vector", lambda e, fs=fs, fgv=fgv: e.tensor_tensor(
                flat(fs), flat(fs), flat(fgv), ALU.mult), waits=[l2, l3])
            fc_m2 = b.op("vector", lambda e, fx=fx, fs=fs: e.tensor_tensor(
                flat(fx), flat(fx), flat(fs), ALU.add), waits=[l1])
            fc_st = b.dma("gpsimd", lambda e, r0f=r0f, fx=fx: e.dma_start(
                rearr_rows(y, r0f, TCA), fx), waits=[fc_m2])

        def row_stream(lhsT_src, w_dram, out_banks4, cast_ms, first_wait,
                       kk_base=0, n_kk=8, stop_kk=None):
            res = []
            nonlocal_rs = rslot_free
            if stop_kk is None:
                stop_kk = n_kk - 1
            for nh in range(2):
                pw = {}
                for m8 in range(2):
                    bank = out_banks4[2 * nh + m8]
                    pw[m8] = [psum_free[bank]] if psum_free[bank] else []
                for kk in range(n_kk):
                    slot = kk % 4
                    ldr = b.dma("sync", lambda e, kk=kk, nh=nh, slot=slot,
                                w_dram=w_dram, kk_base=kk_base: e.dma_start(
                        wrow[slot][:], w_dram.rearrange("(kk p) n -> p kk n", p=P)[
                            :, kk_base + kk, 512 * nh:512 * (nh + 1)]),
                        waits=[cast_ms, nonlocal_rs[slot]])
                    for m8 in range(2):
                        bank = out_banks4[2 * nh + m8]
                        wts_ = []
                        if kk == 0:
                            wts_ = [first_wait] + pw[m8]
                        if m8 == 0:
                            wts_.append(ldr)
                        b.mm(lambda e, kk=kk, m8=m8, nh=nh, bank=bank, slot=slot,
                             lhsT_src=lhsT_src: e.matmul(
                            PS[bank][:], lhsT_src[:, kk, 128 * m8:128 * (m8 + 1)],
                            wrow[slot][:], start=(kk == 0), stop=(kk == stop_kk)),
                            waits=wts_, mark=(m8 == 1))
                        if m8 == 1:
                            nonlocal_rs[slot] = ("pe", b.c["pe"])
                for m8 in range(2):
                    res.append((("pe", b.c["pe"]), out_banks4[2 * nh + m8], m8, nh))
            return res

        for c in range(NCA):
            r0 = c * TCA
            for _ in range(2):
                if cast_i < len(expert_casts):
                    ei, kind = expert_casts[cast_i]
                    src = I["w1"][ei] if kind == "bw1" else I["w2"][ei]
                    dst = S[kind][ei]
                    cast_done[(kind, ei)] = b.dma(
                        "gpsimd", lambda e, s_=src, d=dst: e.dma_start(d, s_))
                    cast_i += 1
                    prog["expert_cast"] = cast_i
            if ld_x0 is None:
                ld_x0 = b.dma("sync", lambda e, r0=r0: e.dma_start(
                    x0c, rearr_rows(I["x"], r0, TCA)), waits=[last_x0_use, dbg_xn])
            yield 2
            xn_ms = []
            for tt in range(2):
                w0 = [ld_x0]
                if tt == 0:
                    w0 += last_s_rd + ([m_init] if c == 0 else [])
                xn_ms.append(ln_tile(x0c, tt, 0, xnc, w0))
            last_x0_use = xn_ms[-1]
            if c + 1 < NCA:
                r1 = (c + 1) * TCA
                ld_x0 = b.dma("sync", lambda e, r1=r1: e.dma_start(
                    x0c, rearr_rows(I["x"], r1, TCA)), waits=[last_x0_use, dbg_xn])
            else:
                ld_x0 = None
            yield 16
            xnb = flat(hsh).rearrange("p (t e) -> p t e", t=2)
            mcast = None
            for tt in range(2):
                mcast = b.op("vector", lambda e, tt=tt, xnb=xnb: e.tensor_copy(
                    xnb[:, tt, :], xnc[:, tt, :]))
            sp_xn = b.dma("sync", lambda e, r0=r0, xnb=xnb: e.dma_start(
                rearr_rows(S["xn_b"], r0, TCA), xnb), waits=[mcast])
            prog["xn_sp"].append(sp_xn)
            if _DEBUG:
                dbg_xn = b.dma("sync", lambda e, r0=r0: e.dma_start(
                    rearr_rows(dbg["d_xn"], r0, TCA), xnc), waits=[xn_ms[-1]])
            prog["xn_rows"] = r0 + TCA
            yield 3
            t1_ms = [ln_tile(xnc, tt, 1, t1b,
                             [t1b_free] if tt == 0 and t1b_free else [])
                     for tt in range(2)]
            yield 14
            _, t1b_free = transpose_in(t1b, hT, g1, b1, t1_ms[-1])
            yield 6
            b.op("vector", lambda e: e.tensor_copy(
                hsh[:, :, 1:TCA], hT[:, :, 0:TCA - 1]), waits=[sp_xn])
            m_s2 = b.op("vector", lambda e: e.tensor_copy(
                hsh[:, :, 0:1], carry1[:].unsqueeze(2)))
            b.op("vector", lambda e: e.tensor_copy(
                carry1[:].unsqueeze(2), hT[:, :, TCA - 1:TCA]))
            yield 3
            evac_ms = {}
            mix_v = {"k": (mk, omk), "v": (mv, omv), "r": (mr, omr)}
            wdram = {"k": S["bWk"], "v": S["bWv"], "r": S["bWr"]}
            wcast = {"k": cast_done["bWk"], "v": cast_done["bWv"], "r": cast_done["bWr"]}
            for pi, pn in enumerate(("k", "v", "r")):
                xm = xm0 if pi % 2 == 0 else xm1
                mvec, ovec = mix_v[pn]
                mmix0 = b.op("vector", lambda e, xm=xm, ovec=ovec: e.tensor_tensor(
                    xm[:], hsh[:], ovec[:].unsqueeze(2).broadcast_to([P, 8, TCA]),
                    ALU.mult), waits=[m_s2, ("pe", b.c["pe"])])
                mmix = None
                for i in range(8):
                    mmix = b.op("vector", lambda e, i=i, xm=xm, mvec=mvec:
                                e.scalar_tensor_tensor(
                        xm[:, i, :], hT[:, i, :], mvec[:, i:i + 1], xm[:, i, :],
                        ALU.mult, ALU.add))
                yield 5
                last_ev = None
                for m8 in range(8):
                    slot = m8 % 4
                    ldw = b.dma("sync", lambda e, pn=pn, m8=m8, slot=slot: e.dma_start(
                        wtile[slot][:], wdram[pn][:, 128 * m8:128 * (m8 + 1)].rearrange(
                            "(kk p) n -> p kk n", p=P)),
                        waits=[wcast[pn], wslot_free[slot]])
                    bank = (AB + AP_)[m8 % 4]
                    wts_ = [ldw]
                    if m8 == 0:
                        wts_.append(mmix)
                    if psum_free[bank]:
                        wts_.append(psum_free[bank])
                    for kk in range(8):
                        b.mm(lambda e, kk=kk, slot=slot, bank=bank, xm=xm: e.matmul(
                            PS[bank][:, 0:TCA], wtile[slot][:, kk, :], xm[:, kk, :],
                            start=(kk == 0), stop=(kk == 7)),
                            waits=wts_ if kk == 0 else [], mark=(kk == 7))
                    gm = ("pe", b.c["pe"])
                    wslot_free[slot] = gm
                    if pn == "k":
                        ev = b.op("vector", lambda e, m8=m8, bank=bank: e.tensor_copy(
                            kT[:, m8, :], PS[bank][:, 0:TCA]),
                            waits=[gm, dbg_wkv, fc_st] if m8 == 0 else [gm])
                    elif pn == "v":
                        ev = b.op("vector", lambda e, m8=m8, bank=bank: e.tensor_copy(
                            vT[:, m8, :], PS[bank][:, 0:TCA]), waits=[gm])
                    else:
                        ev = b.op("scalar", lambda e, m8=m8, bank=bank: e.activation(
                            xm1[:, m8, :], PS[bank][:, 0:TCA], AF.Sigmoid), waits=[gm])
                    psum_free[bank] = ev
                    last_ev = ev
                    if m8 % 2 == 1:
                        yield 2.2
                evac_ms[pn] = last_ev
            dbg_k = None
            if _DEBUG:
                dbg_k = b.dma("sync", lambda e, r0=r0: e.dma_start(
                    dbg["d_k"].rearrange("(i p) t -> p i t", p=P)[:, :, r0:r0 + TCA], kT),
                    waits=[evac_ms["k"]])
            m_ng = b.op("vector", lambda e: e.tensor_reduce(
                ng[:].unsqueeze(2), kT, AX.X, ALU.max, negate=True), waits=[evac_ms["k"]])
            m_ug = b.op("vector", lambda e: e.tensor_tensor(ug[:], uvec[:], ng[:], ALU.add))
            m_ec = None
            for i in range(8):
                m_ec = b.op("scalar", lambda e, i=i: e.activation(
                    ecb[:, i, :], kT[:, i, :], AF.Exp, bias=ng[:, i:i + 1]),
                    waits=([m_ng] if i == 0 else []))
            m_za = b.op("vector", lambda e: e.tensor_tensor(zab, ecb, vT[:], ALU.mult),
                        waits=[m_ec, evac_ms["v"]])
            m_sa = b.op("vector", lambda e: e.tensor_tensor_scan(
                flat(sab), ew0[:], flat(zab), 0.0, ALU.mult, ALU.add),
                waits=[m_ew0b, m_ew0a])
            m_sb = b.op("vector", lambda e: e.tensor_tensor_scan(
                flat(zab), ew0[:], flat(ecb), 0.0, ALU.mult, ALU.add))
            yield 12
            m_t = b.op("vector", lambda e: e.tensor_tensor(sc8[:], pst[:], ng[:], ALU.add))
            m_eP = b.op("scalar", lambda e: e.activation(sc8[:], sc8[:], AF.Exp),
                        waits=[m_t])
            b.op("vector", lambda e: e.tensor_tensor(aag[:], aa[:], sc8[:], ALU.mult),
                 waits=[m_eP])
            b.op("vector", lambda e: e.tensor_tensor(bbg[:], bb[:], sc8[:], ALU.mult))
            b.op("vector", lambda e: e.tensor_tensor(aa[:], ecw[:], aag[:], ALU.mult),
                 waits=[m_ecw])
            b.op("vector", lambda e: e.tensor_tensor(
                aa[:].unsqueeze(2), aa[:].unsqueeze(2), sab[:, :, TCA - 1:TCA], ALU.add))
            b.op("vector", lambda e: e.tensor_tensor(bb[:], ecw[:], bbg[:], ALU.mult))
            b.op("vector", lambda e: e.tensor_tensor(
                bb[:].unsqueeze(2), bb[:].unsqueeze(2), zab[:, :, TCA - 1:TCA], ALU.add))
            b.op("vector", lambda e: e.tensor_scalar(
                pst[:], ng[:], -1.0, None, ALU.mult))
            m_es = None
            for i in range(8):
                m_es = b.op("scalar", lambda e, i=i: e.activation(
                    ecb[:, i, :], kT[:, i, :], AF.Exp, bias=ug[:, i:i + 1]),
                    waits=([m_sb, m_ug] if i == 0 else []))
            yield 8
            b.op("vector", lambda e: e.tensor_tensor(
                kT, ejw3, aag[:].unsqueeze(2).broadcast_to([P, 8, TCA]), ALU.mult),
                waits=[m_es, m_ejw, dbg_k])
            b.op("vector", lambda e: e.tensor_tensor(
                kT[:, :, 1:], kT[:, :, 1:], sab[:, :, 0:TCA - 1], ALU.add))
            b.op("vector", lambda e: e.tensor_tensor(vT[:], ecb, vT[:], ALU.mult))
            b.op("vector", lambda e: e.tensor_tensor(kT, kT, vT[:], ALU.add))
            b.op("vector", lambda e: e.tensor_tensor(
                sab, ejw3, bbg[:].unsqueeze(2).broadcast_to([P, 8, TCA]), ALU.mult))
            b.op("vector", lambda e: e.tensor_tensor(
                sab[:, :, 1:], sab[:, :, 1:], zab[:, :, 0:TCA - 1], ALU.add))
            m_d3 = b.op("vector", lambda e: e.tensor_tensor(sab, sab, ecb, ALU.add))
            b.op("scalar", lambda e: e.activation(flat(sab), flat(sab), AF.Ln),
                 waits=[m_d3])
            m_r2 = b.op("scalar", lambda e: e.activation(flat(sab), flat(sab), AF.Exp,
                                                         scale=-1.0))
            m_wkv = b.op("vector", lambda e: e.tensor_tensor(kT, kT, sab, ALU.mult),
                         waits=[m_r2])
            if _DEBUG:
                dbg_wkv = b.dma("sync", lambda e, r0=r0: e.dma_start(
                    dbg["d_wkv"].rearrange("(i p) t -> p i t", p=P)[:, :, r0:r0 + TCA], kT),
                    waits=[m_wkv])
            m_rk = b.op("vector", lambda e: e.tensor_tensor(xm0[:], kT, xm1[:], ALU.mult),
                        waits=[evac_ms["r"], ("pe", b.c["pe"])])
            last_mrk = m_rk
            yield 16
            groups = row_stream(xm0, S["bWo"], AB + AP_, cast_done["bWo"], m_rk)
            x1_ms = []
            for (gm_, bank, m8, nh) in groups:
                ev = b.op("vector", lambda e, m8=m8, nh=nh, bank=bank: e.tensor_tensor(
                    xnc[:, m8, 512 * nh:512 * (nh + 1)],
                    xnc[:, m8, 512 * nh:512 * (nh + 1)], PS[bank][:], ALU.add),
                    waits=[gm_])
                psum_free[bank] = ev
                x1_ms.append(ev)
            yield 8
            drained = 0
            while (fc_next <= c - 2 and drained < 2
                   and prog["g_rows"] >= fc_next * TCA + TCA):
                emit_fc(fc_next, m_rk)
                fc_next += 1
                drained += 1
                yield 5
            if _DEBUG:
                b.dma("sync", lambda e, r0=r0: e.dma_start(
                    rearr_rows(dbg["d_x1"], r0, TCA), xnc), waits=[x1_ms[-1]])
            t2_ms = [ln_tile(xnc, tt, 1, t1b,
                             [x1_ms[-1], t1b_free] if tt == 0 else [])
                     for tt in range(2)]
            yield 14
            _, t1b_free = transpose_in(t1b, hT, g2, b2, t2_ms[-1])
            yield 6
            b.op("vector", lambda e: e.tensor_copy(hsh[:, :, 1:TCA], hT[:, :, 0:TCA - 1]))
            m2_s2 = b.op("vector", lambda e: e.tensor_copy(
                hsh[:, :, 0:1], carry2[:].unsqueeze(2)))
            b.op("vector", lambda e: e.tensor_copy(
                carry2[:].unsqueeze(2), hT[:, :, TCA - 1:TCA]))
            yield 3
            b.op("vector", lambda e: e.tensor_tensor(
                xm1[:], hsh[:], ocmr[:].unsqueeze(2).broadcast_to([P, 8, TCA]), ALU.mult),
                waits=[m2_s2, ("pe", b.c["pe"])])
            mmxr = None
            for i in range(8):
                mmxr = b.op("vector", lambda e, i=i: e.scalar_tensor_tensor(
                    xm1[:, i, :], hT[:, i, :], cmr[:, i:i + 1], xm1[:, i, :],
                    ALU.mult, ALU.add))
            yield 5
            groups = row_stream(xm1, S["bcWr"], AB + AP_, cast_done["bcWr"], mmxr)
            for (gm_, bank, m8, nh) in groups:
                ev = b.op("scalar", lambda e, m8=m8, nh=nh, bank=bank: e.activation(
                    rc2[:, m8, 512 * nh:512 * (nh + 1)], PS[bank][:], AF.Sigmoid),
                    waits=[gm_])
                psum_free[bank] = ev
            rc_mark = ("pe", b.c["pe"])
            yield 8
            b.op("vector", lambda e: e.tensor_tensor(
                xm1[:], hsh[:], ocmk[:].unsqueeze(2).broadcast_to([P, 8, TCA]), ALU.mult),
                waits=[rc_mark])
            mmxk = None
            for i in range(8):
                mmxk = b.op("vector", lambda e, i=i: e.scalar_tensor_tensor(
                    xm1[:, i, :], hT[:, i, :], cmk[:, i:i + 1], xm1[:, i, :],
                    ALU.mult, ALU.add))
            yield 5
            s_ms = []
            rel_free = [None, None]
            for fh in range(2):
                sq_ms = None
                for fm in range(16):
                    fmg = 16 * fh + fm
                    slot = fm % 4
                    ldt = b.dma("sync", lambda e, fmg=fmg, slot=slot: e.dma_start(
                        wtile[slot][:], S["bcWk"][:, 128 * fmg:128 * (fmg + 1)].rearrange(
                            "(kk p) n -> p kk n", p=P)),
                        waits=[cast_done["bcWk"], wslot_free[slot]])
                    bank = AP_[fm % 2]
                    wts_ = [ldt]
                    if fm == 0:
                        wts_.append(mmxk)
                        if fh == 0 and kc_free:
                            wts_.append(kc_free)
                    if psum_free[bank]:
                        wts_.append(psum_free[bank])
                    for kk in range(8):
                        b.mm(lambda e, kk=kk, slot=slot, bank=bank: e.matmul(
                            PS[bank][:, 0:TCA], wtile[slot][:, kk, :], xm1[:, kk, :],
                            start=(kk == 0), stop=(kk == 7)),
                            waits=wts_ if kk == 0 else [], mark=(kk == 7))
                    gm = ("pe", b.c["pe"])
                    wslot_free[slot] = gm
                    rel = relA if fm % 2 == 0 else relB
                    evr = b.op("scalar", lambda e, bank=bank, rel=rel: e.activation(
                        rel[:], PS[bank][:, 0:TCA], AF.Relu),
                        waits=[gm, rel_free[fm % 2]])
                    psum_free[bank] = evr
                    sq_ms = b.op("vector", lambda e, fm=fm, rel=rel: e.tensor_tensor(
                        kcH[:, fm, :], rel[:], rel[:], ALU.mult), waits=[evr])
                    rel_free[fm % 2] = sq_ms
                    if fm % 2 == 1:
                        yield 2.2
                groups = row_stream(kcH, S["bcWv"], AB + AP_, cast_done["bcWv"], sq_ms,
                                    kk_base=16 * fh, n_kk=16, stop_kk=15)
                for (gm_, bank, m8, nh) in groups:
                    mt = b.op("vector", lambda e, m8=m8, nh=nh, bank=bank: e.tensor_tensor(
                        tmp5[:], PS[bank][:], rc2[:, m8, 512 * nh:512 * (nh + 1)],
                        ALU.mult), waits=[gm_])
                    ma = b.op("vector", lambda e, m8=m8, nh=nh: e.tensor_tensor(
                        xnc[:, m8, 512 * nh:512 * (nh + 1)],
                        xnc[:, m8, 512 * nh:512 * (nh + 1)], tmp5[:], ALU.add))
                    psum_free[bank] = ma
                    s_ms.append(ma)
                kc_free = ("pe", b.c["pe"])
                yield 14
            sp_s = b.dma("sync", lambda e, r0=r0: e.dma_start(
                rearr_rows(S["s_f"], r0, TCA), xnc), waits=[s_ms[-1]])
            last_s_rd = [sp_s]
            if _DEBUG:
                last_s_rd.append(b.dma("sync", lambda e, r0=r0: e.dma_start(
                    rearr_rows(dbg["d_s"], r0, TCA), xnc), waits=[s_ms[-1]]))
            prog["s_sp"].append(sp_s)
            prog["s_rows"] = r0 + TCA
            yield 3
        while fc_next < NCA:
            while prog["g_rows"] < fc_next * TCA + TCA:
                yield ("stall",)
            emit_fc(fc_next, last_mrk)
            fc_next += 1
            yield 5

    def gen_F():
        last_xnT = None
        last_gout = None
        last_gacc = None
        w2_free = [None, None]
        w1_free = [None, None, None, None]

        for cf in range(NCF):
            r0 = cf * TCF
            while prog["xn_rows"] < r0 + TCF:
                yield ("stall",)
            xn_sp_w = [prog["xn_sp"][2 * cf], prog["xn_sp"][2 * cf + 1]]
            xnT_lds = []
            for i in range(8):
                xnT_lds.append(b.dma("scalar", lambda e, r0=r0, i=i: e.dma_start(
                    xnT[:, i, :], S["xn_b"][r0:r0 + TCF, 128 * i:128 * (i + 1)],
                    transpose=True),
                    waits=([last_xnT, last_gout] + xn_sp_w) if i == 0 else []))
            yield 8
            lg = None
            for m8 in range(4):
                bank = FC_[0]
                wts_ = (xnT_lds + [m_rwc]) if m8 == 0 else []
                if psum_free[bank]:
                    wts_ = wts_ + [psum_free[bank]]
                for kk in range(8):
                    b.mm(lambda e, m8=m8, kk=kk, bank=bank: e.matmul(
                        PS[bank][:, 0:NE], xnT[:, kk, 128 * m8:128 * (m8 + 1)],
                        rwT[:, kk, :], start=(kk == 0), stop=(kk == 7)),
                        waits=wts_, mark=(kk == 7))
                gm = ("pe", b.c["pe"])
                lg = b.op("vector", lambda e, m8=m8, bank=bank: e.tensor_copy(
                    logit[:, m8, :], PS[bank][:, 0:NE]), waits=[gm])
                psum_free[bank] = lg
            yield 4
            b.op("vector", lambda e: e.tensor_reduce(
                m1n[:].unsqueeze(2), logit[:], AX.X, ALU.max, negate=True), waits=[lg])
            b.op("vector", lambda e: e.tensor_tensor(
                tkA[:], logit[:], m1n[:].unsqueeze(2).broadcast_to([P, 4, NE]), ALU.add))
            b.op("vector", lambda e: e.tensor_scalar(
                tkB[:], tkA[:], 0.0, None, ALU.is_equal))
            b.op("vector", lambda e: e.scalar_tensor_tensor(
                tkA[:], tkB[:], -1e30, logit[:], ALU.mult, ALU.add))
            b.op("vector", lambda e: e.tensor_reduce(
                m2n[:].unsqueeze(2), tkA[:], AX.X, ALU.max, negate=True))
            b.op("vector", lambda e: e.tensor_tensor(
                tkA[:], tkA[:], m2n[:].unsqueeze(2).broadcast_to([P, 4, NE]), ALU.add))
            b.op("vector", lambda e: e.tensor_scalar(
                tkA[:], tkA[:], 0.0, None, ALU.is_equal))
            mH = b.op("vector", lambda e: e.tensor_tensor(tkB[:], tkB[:], tkA[:], ALU.add))
            mI = None
            for m8 in range(4):
                mI = b.op("scalar", lambda e, m8=m8: e.activation(
                    wts[:, m8, :], logit[:, m8, :], AF.Exp, bias=m1n[:, m8:m8 + 1]),
                    waits=([mH] if m8 == 0 else []))
            b.op("vector", lambda e: e.tensor_tensor(wts[:], wts[:], tkB[:], ALU.mult),
                 waits=[mI])
            b.op("vector", lambda e: e.tensor_reduce(
                zsum[:].unsqueeze(2), wts[:], AX.X, ALU.add))
            b.op("vector", lambda e: e.reciprocal(zsum[:], zsum[:]))
            mW = b.op("vector", lambda e: e.tensor_tensor(
                wts[:], wts[:], zsum[:].unsqueeze(2).broadcast_to([P, 4, NE]), ALU.mult))
            mz = b.op("vector", lambda e: e.memset(flat(gacc), 0.0), waits=[last_gacc])
            yield 6
            for ei in range(NE):
                while prog["expert_cast"] < 2 * (ei + 1):
                    yield ("stall",)
                for h in range(2):
                    slot2 = h
                    w2ld = b.dma("scalar", lambda e, ei=ei, h=h, slot2=slot2: e.dma_start(
                        w2s[slot2][:], S["bw2"][ei][1024 * h:1024 * (h + 1), :].rearrange(
                            "(kk p) n -> p kk n", p=P)),
                        waits=[cast_done[("bw2", ei)], w2_free[slot2]])
                    he_last = None
                    w1_lds = {}

                    def emit_w1(m8, ei=ei, h=h, w1_lds=w1_lds):
                        slot = m8 % 4
                        col = 1024 * h + 128 * m8
                        w1_lds[m8] = b.dma("scalar", lambda e, ei=ei, col=col, slot=slot:
                                           e.dma_start(
                            w1t[slot][:], S["bw1"][ei][:, col:col + 128].rearrange(
                                "(kk p) n -> p kk n", p=P)),
                            waits=[cast_done[("bw1", ei)], w1_free[slot]])

                    emit_w1(0)
                    emit_w1(1)
                    for m8 in range(8):
                        slot = m8 % 4
                        bank = FB[m8 % 2]
                        wts_ = [w1_lds[m8]]
                        if psum_free[bank]:
                            wts_.append(psum_free[bank])
                        for kk in range(8):
                            b.mm(lambda e, kk=kk, slot=slot, bank=bank: e.matmul(
                                PS[bank][:], w1t[slot][:, kk, :], xnT[:, kk, :],
                                start=(kk == 0), stop=(kk == 7)),
                                waits=wts_ if kk == 0 else [], mark=(kk == 7))
                            if kk == 0 and m8 + 2 < 8:
                                emit_w1(m8 + 2)
                        gm = ("pe", b.c["pe"])
                        w1_free[slot] = gm
                        he_last = b.op("scalar", lambda e, m8=m8, bank=bank: e.activation(
                            heT[:, m8, :], PS[bank][:], AF.Gelu), waits=[gm])
                        psum_free[bank] = he_last
                        yield 1.9
                    for m8 in range(4):
                        for nh in range(2):
                            bank = FC_[nh % 2]
                            wts_ = []
                            if m8 == 0 and nh == 0:
                                wts_ = [he_last, w2ld]
                            if psum_free[bank]:
                                wts_ = wts_ + [psum_free[bank]]
                            for kk2 in range(8):
                                b.mm(lambda e, kk2=kk2, m8=m8, nh=nh, slot2=slot2,
                                     bank=bank: e.matmul(
                                    PS[bank][:], heT[:, kk2, 128 * m8:128 * (m8 + 1)],
                                    w2s[slot2][:, kk2, 512 * nh:512 * (nh + 1)],
                                    start=(kk2 == 0), stop=(kk2 == 7)),
                                    waits=wts_, mark=(kk2 == 7))
                            gm = ("pe", b.c["pe"])
                            ev = b.op("vector", lambda e, m8=m8, nh=nh, ei=ei, bank=bank:
                                      e.scalar_tensor_tensor(
                                gacc[:, m8, 512 * nh:512 * (nh + 1)], PS[bank][:],
                                wts[:, m8, ei:ei + 1],
                                gacc[:, m8, 512 * nh:512 * (nh + 1)],
                                ALU.mult, ALU.add),
                                waits=[gm] + ([mW, mz] if m8 == 0 and nh == 0 else []))
                            psum_free[bank] = ev
                            yield 1.9
                    w2_free[slot2] = ("pe", b.c["pe"])
            gdone = ("v", b.c["v"])
            m_g = b.op("scalar", lambda e: e.activation(flat(gout), flat(gacc), AF.Silu),
                       waits=[gdone, last_gout])
            last_gacc = m_g
            sp_g = b.dma("scalar", lambda e, r0=r0: e.dma_start(
                rearr_rows(S["g_b"], r0, TCF), gout), waits=[m_g])
            last_gout = sp_g
            last_xnT = ("pe", b.c["pe"])
            dbg_g = None
            if _DEBUG:
                dbf = b.op("vector", lambda e: e.tensor_copy(flat(gacc), flat(gout)),
                           waits=[m_g])
                dbg_g = b.dma("sync", lambda e, r0=r0: e.dma_start(
                    rearr_rows(dbg["d_g"], r0, TCF), gacc), waits=[dbf])
                last_gacc = dbg_g
            yield 4
            prog["g_sp"].append(sp_g)
            prog["g_rows"] = r0 + TCF
            last_gacc = m_g if not _DEBUG else last_gacc

    ag = gen_A()
    fg = gen_F()
    tA = 0.0
    tF = 0.0
    a_alive = f_alive = True
    a_stalled = f_stalled = False
    while a_alive or f_alive:
        if a_stalled and f_stalled:
            raise RuntimeError("driver deadlock: both streams stalled")
        if not f_alive:
            pull_a = True
        elif not a_alive:
            pull_a = False
        elif f_stalled:
            pull_a = True
        elif a_stalled:
            pull_a = False
        else:
            pull_a = tA <= tF
        if pull_a:
            try:
                r = next(ag)
                if isinstance(r, tuple):
                    a_stalled = True
                else:
                    tA += r
                    f_stalled = False
            except StopIteration:
                a_alive = False
                tA = float("inf")
                f_stalled = False
        else:
            try:
                r = next(fg)
                if isinstance(r, tuple):
                    f_stalled = True
                else:
                    tF += r
                    a_stalled = False
            except StopIteration:
                f_alive = False
                tF = float("inf")
                a_stalled = False

    fin = (b.queue_done_waits("sync") + b.queue_done_waits("scalar")
           + b.queue_done_waits("gpsimd"))

    def _final(e, fin=fin):
        for (k, v) in fin:
            e.wait_ge(sems[k], v)
    b.raw("sync", _final)

    with nc.Block() as block:
        block.sync(lambda e: [f(e) for f in b.q["sync"]])
        block.gpsimd(lambda e: [f(e) for f in b.q["gpsimd"]])
        block.scalar(lambda e: [f(e) for f in b.q["scalar"]])
        block.vector(lambda e: [f(e) for f in b.q["vector"]])
        block.tensor(lambda e: [f(e) for f in b.q["tensor"]])

    for cm in reversed(sem_cms):
        cm.__exit__(None, None, None)
    return nc


_NC_CACHE = None


def kernel(**inputs):
    global _NC_CACHE
    if _NC_CACHE is None:
        _NC_CACHE = build_nc()
    nc = _NC_CACHE

    x = np.ascontiguousarray(np.asarray(inputs["L0_attn_inputs"], np.float32))
    base = {
        "ln1g": np.asarray(inputs["L0_attn_ln1_g"], np.float32),
        "ln1b": np.asarray(inputs["L0_attn_ln1_b"], np.float32),
        "ln2g": np.asarray(inputs["L0_attn_ln2_g"], np.float32),
        "ln2b": np.asarray(inputs["L0_attn_ln2_b"], np.float32),
        "mixk": np.asarray(inputs["L0_attn_tm_mix_k"], np.float32),
        "mixv": np.asarray(inputs["L0_attn_tm_mix_v"], np.float32),
        "mixr": np.asarray(inputs["L0_attn_tm_mix_r"], np.float32),
        "cmixk": np.asarray(inputs["L0_attn_cm_mix_k"], np.float32),
        "cmixr": np.asarray(inputs["L0_attn_cm_mix_r"], np.float32),
        "td": np.asarray(inputs["L0_attn_time_decay"], np.float32),
        "tf": np.asarray(inputs["L0_attn_time_first"], np.float32),
        "Wk": np.asarray(inputs["L0_attn_tm_Wk"], np.float32),
        "Wv": np.asarray(inputs["L0_attn_tm_Wv"], np.float32),
        "Wr": np.asarray(inputs["L0_attn_tm_Wr"], np.float32),
        "Wo": np.asarray(inputs["L0_attn_Wo"], np.float32),
        "cWk": np.asarray(inputs["L0_attn_cm_Wk"], np.float32),
        "cWv": np.asarray(inputs["L0_attn_cm_Wv"], np.float32),
        "cWr": np.asarray(inputs["L0_attn_cm_Wr"], np.float32),
        "rw": np.asarray(inputs["L0_ffn_router_w"], np.float32),
        "w1": np.asarray(inputs["L0_ffn_w1"], np.float32),
        "w2": np.asarray(inputs["L0_ffn_w2"], np.float32),
    }
    in_maps = [dict(base, x=x[bi]) for bi in range(8)]
    res = run_bass_kernel_spmd(nc, in_maps, core_ids=list(range(8)))
    out = np.stack([res.results[bi]["y"] for bi in range(8)], axis=0).astype(np.float32)
    if _DEBUG:
        kernel.debug_results = res.results
    return out


# version 10
# speedup vs baseline: 1.0591x; 1.0198x over previous
import numpy as np

import concourse.bass as bass
import concourse.mybir as mybir
from concourse.bass_utils import run_bass_kernel_spmd

F32 = mybir.dt.float32
BF16 = mybir.dt.bfloat16
AF = mybir.ActivationFunctionType
ALU = mybir.AluOpType
AX = mybir.AxisListType

P = 128
T = 4096
E = 1024
H = 1024
F4 = 4096
FE = 2048
NE = 8

TCA = 256
NCA = T // TCA
TCF = 512
NCF = T // TCF

LNEPS_OUT = 1e-5
LNEPS_IN = 1e-3
P_INIT = -1e30

_DEBUG = False


class Bld:
    ENGS = ("sync", "gpsimd", "scalar", "vector", "tensor")
    SELF = {"vector": "v", "scalar": "a", "gpsimd": "p"}
    NQSEM = 8

    def __init__(self, nc, sems):
        self.nc = nc
        self.sems = sems
        self.q = {e: [] for e in self.ENGS}
        self.c = {k: 0 for k in ("v", "a", "p", "pe")}
        for base in ("sq", "wq", "gq"):
            self.c[base] = 0
            for i in range(self.NQSEM):
                self.c[f"{base}{i}"] = 0

    def _mk_waits(self, waits):
        out = []
        for w in waits:
            if w is None:
                continue
            (s, val) = w
            if val > 0:
                out.append((self.sems[s], val))
        return out

    def op(self, eng, fn, waits=()):
        sc = self.SELF.get(eng)
        w = self._mk_waits(waits)
        if sc is not None:
            k = self.c[sc]
            sem = self.sems[sc]

            def run(e, fn=fn, w=w, k=k, sem=sem):
                for (s, val) in w:
                    e.wait_ge(s, val)
                if k > 0:
                    e.wait_ge(sem, k)
                fn(e).then_inc(sem, 1)

            self.q[eng].append(run)
            self.c[sc] = k + 1
            return (sc, k + 1)
        else:
            def run(e, fn=fn, w=w):
                for (s, val) in w:
                    e.wait_ge(s, val)
                fn(e)

            self.q[eng].append(run)
            return None

    def mm(self, fn, waits=(), mark=False):
        w = self._mk_waits(waits)
        if mark:
            self.c["pe"] += 1
            pe_sem = self.sems["pe"]

            def run(e, fn=fn, w=w, pe_sem=pe_sem):
                for (s, val) in w:
                    e.wait_ge(s, val)
                fn(e).then_inc(pe_sem, 1)

            self.q["tensor"].append(run)
            return ("pe", self.c["pe"])
        else:
            def run(e, fn=fn, w=w):
                for (s, val) in w:
                    e.wait_ge(s, val)
                fn(e)

            self.q["tensor"].append(run)
            return None

    def dma(self, queue, fn, waits=()):
        base = {"sync": "sq", "scalar": "wq", "gpsimd": "gq"}[queue]
        idx = self.c[base] % self.NQSEM
        key = f"{base}{idx}"
        sem = self.sems[key]
        prev_done = self.c[key]
        self.c[base] += 1
        self.c[key] = prev_done + 16
        w = self._mk_waits(waits)

        def run(e, fn=fn, w=w, sem=sem, prev_done=prev_done):
            if prev_done > 0:
                e.wait_ge(sem, prev_done)
            for (s, val) in w:
                e.wait_ge(s, val)
            fn(e).then_inc(sem, 16)

        self.q[queue].append(run)
        return (key, self.c[key])

    def queue_done_waits(self, queue):
        base = {"sync": "sq", "scalar": "wq", "gpsimd": "gq"}[queue]
        return [(f"{base}{i}", self.c[f"{base}{i}"]) for i in range(self.NQSEM)
                if self.c.get(f"{base}{i}", 0) > 0]

    def raw(self, eng, fn):
        self.q[eng].append(fn)


def rearr_rows(dram_ap, r0, rows):
    return dram_ap[r0:r0 + rows, :].rearrange("(t p) e -> p t e", p=P)


def build_nc():
    nc = bass.Bass("TRN2", target_bir_lowering=False, debug=False)

    I = {}
    I["x"] = nc.dram_tensor("x", [T, E], F32, kind="ExternalInput").ap()
    for nm, shp in [("ln1g", [E]), ("ln1b", [E]), ("ln2g", [E]), ("ln2b", [E]),
                    ("mixk", [E]), ("mixv", [E]), ("mixr", [E]),
                    ("cmixk", [E]), ("cmixr", [E]), ("td", [H]), ("tf", [H])]:
        I[nm] = nc.dram_tensor(nm, shp, F32, kind="ExternalInput").ap()
    for nm, shp in [("Wk", [E, H]), ("Wv", [E, H]), ("Wr", [E, H]), ("Wo", [H, E]),
                    ("cWk", [E, F4]), ("cWv", [F4, E]), ("cWr", [E, E]),
                    ("rw", [NE, E]), ("w1", [NE, E, FE]), ("w2", [NE, FE, E])]:
        I[nm] = nc.dram_tensor(nm, shp, F32, kind="ExternalInput").ap()

    y = nc.dram_tensor("y", [T, E], F32, kind="ExternalOutput").ap()

    S = {}
    for nm, shp in [("bWk", [E, H]), ("bWv", [E, H]), ("bWr", [E, H]), ("bWo", [H, E]),
                    ("bcWk", [E, F4]), ("bcWv", [F4, E]), ("bcWr", [E, E]),
                    ("bw1", [NE, E, FE]), ("bw2", [NE, FE, E]),
                    ("xn_b", [T, E]), ("g_b", [T, E])]:
        S[nm] = nc.dram_tensor(nm, shp, BF16, kind="Internal").ap()
    S["s_f"] = nc.dram_tensor("s_f", [T, E], F32, kind="Internal").ap()

    dbg = {}
    if _DEBUG:
        for nm, shp in [("d_xn", [T, E]), ("d_k", [H, T]), ("d_wkv", [H, T]),
                        ("d_x1", [T, E]), ("d_s", [T, E]), ("d_g", [T, E])]:
            dbg[nm] = nc.dram_tensor(nm, shp, F32, kind="ExternalOutput").ap()

    def sb(name, shape, dt):
        return nc.alloc_sbuf_tensor(name, shape, dt)

    g1 = sb("g1", [P, 8], F32); b1 = sb("b1", [P, 8], F32)
    g2 = sb("g2", [P, 8], F32); b2 = sb("b2", [P, 8], F32)
    mk = sb("mk", [P, 8], F32); mv = sb("mv", [P, 8], F32); mr = sb("mr", [P, 8], F32)
    omk = sb("omk", [P, 8], F32); omv = sb("omv", [P, 8], F32); omr = sb("omr", [P, 8], F32)
    cmk = sb("cmk", [P, 8], F32); cmr = sb("cmr", [P, 8], F32)
    ocmk = sb("ocmk", [P, 8], F32); ocmr = sb("ocmr", [P, 8], F32)
    uvec = sb("uvec", [P, 8], F32)
    wdec = sb("wdec", [P, 8], F32)
    ew = sb("ew", [P, 8], F32)
    ecw = sb("ecw", [P, 8], F32)
    ew0 = sb("ew0", [P, 8 * TCA], F32)
    ejw = sb("ejw", [P, 8 * TCA], F32)
    aa = sb("aa", [P, 8], F32); bb = sb("bb", [P, 8], F32); pst = sb("pst", [P, 8], F32)
    aag = sb("aag", [P, 8], F32); bbg = sb("bbg", [P, 8], F32)
    ng = sb("ng", [P, 8], F32); ug = sb("ug", [P, 8], F32)
    sc8 = sb("sc8", [P, 8], F32)
    carry1 = sb("carry1", [P, 8], BF16)
    carry2 = sb("carry2", [P, 8], BF16)
    stat = sb("stat", [P, 12], F32)
    epsT = sb("epsT", [P, 2], F32)
    idenB = sb("idenB", [P, 128], BF16)
    onesB = sb("onesB", [P, 128], BF16)
    rwT = sb("rwT", [P, 8, NE], BF16)
    rwTf = sb("rwTf", [P, 8, NE], F32)
    logit = sb("logit", [P, 4, NE], F32)
    wts = sb("wts", [P, 4, NE], F32)
    tkA = sb("tkA", [P, 4, NE], F32)
    tkB = sb("tkB", [P, 4, NE], F32)
    m1n = sb("m1n", [P, 4], F32)
    m2n = sb("m2n", [P, 4], F32)
    zsum = sb("zsum", [P, 4], F32)

    x0c = sb("x0c", [P, 2, E], F32)[:]
    xnc = sb("xnc", [P, 2, E], F32)[:]
    kT = sb("kT", [P, 8, TCA], F32)[:]
    ecb = sb("ecb", [P, 8, TCA], F32)[:]
    zab = sb("zab", [P, 8, TCA], F32)[:]
    sab = sb("sab", [P, 8, TCA], F32)[:]
    relA = sb("relA", [P, TCA], F32)[:]
    relB = sb("relB", [P, TCA], F32)[:]
    tmp5 = sb("tmp5", [P, 512], F32)[:]
    t1b = sb("t1b", [P, 2, E], BF16)[:]
    hT = sb("hT", [P, 8, TCA], BF16)[:]
    hsh = sb("hsh", [P, 8, TCA], BF16)[:]
    xm0 = sb("xm0", [P, 8, TCA], BF16)[:]
    xm1 = sb("xm1", [P, 8, TCA], BF16)[:]
    vT = sb("vT", [P, 8, TCA], BF16)[:]
    rc2 = sb("rc2", [P, 2, E], BF16)[:]
    kcH = sb("kcH", [P, 16, TCA], BF16)[:]
    wtile = [sb(f"wtile{i}", [P, 8, 128], BF16) for i in range(4)]
    wrow = [sb(f"wrow{i}", [P, 512], BF16) for i in range(4)]

    gacc = sb("gacc", [P, 4, E], F32)[:]
    xnT = sb("xnT", [P, 8, TCF], BF16)[:]
    heT = sb("heT", [P, 8, TCF], BF16)[:]
    w1t = [sb(f"w1t{i}", [P, 8, 128], BF16) for i in range(4)]
    w2s = [sb(f"w2s{i}", [P, 8, H], BF16) for i in range(2)]
    gout = sb("gout", [P, 4, E], BF16)[:]

    PS = [nc.alloc_psum_tensor(f"ps{i}", [P, 512], F32) for i in range(8)]
    FB = [0, 1]
    FC_ = [2, 3]
    AB = [4, 5]
    AP_ = [6, 7]

    sems = {}
    sem_cms = []
    sem_names = ["v", "a", "p", "pe"]
    for base in ("sq", "wq", "gq"):
        for i in range(Bld.NQSEM):
            sem_names.append(f"{base}{i}")
    for k in sem_names:
        cm = nc.semaphore(f"sem_{k}")
        sems[k] = cm.__enter__()
        sem_cms.append(cm)

    b = Bld(nc, sems)
    psum_free = {i: None for i in range(8)}
    flat = lambda ap3: ap3.rearrange("p a b -> p (a b)")

    cast_done = {}
    for src, dst in [("Wk", "bWk"), ("Wv", "bWv"), ("Wr", "bWr"), ("Wo", "bWo"),
                     ("cWr", "bcWr"), ("cWk", "bcWk"), ("cWv", "bcWv")]:
        cast_done[dst] = b.dma("gpsimd", lambda e, s_=I[src], d=S[dst]: e.dma_start(d, s_))
    expert_casts = [(ei, kind) for ei in range(NE) for kind in ("bw1", "bw2")]

    def vec_load(dst, src):
        return b.dma("sync", lambda e, d=dst, s_=src: e.dma_start(
            d[:], s_.rearrange("(i p) -> p i", p=P), allow_slow_non_contiguous=True))

    pv = []
    for dst, src in [(g1, "ln1g"), (b1, "ln1b"), (g2, "ln2g"), (b2, "ln2b"),
                     (mk, "mixk"), (mv, "mixv"), (mr, "mixr"),
                     (cmk, "cmixk"), (cmr, "cmixr"), (uvec, "tf"), (sc8, "td")]:
        pv.append(vec_load(dst, I[src]))
    rw_load = b.dma("sync", lambda e: e.dma_start(
        rwTf[:], I["rw"].rearrange("e (i p) -> p e i", p=P),
        allow_slow_non_contiguous=True))

    b.op("vector", lambda e: e.memset(epsT[:, 0:1], LNEPS_OUT), waits=pv + [rw_load])
    b.op("vector", lambda e: e.memset(epsT[:, 1:2], LNEPS_IN))
    for mvv, om in ((mk, omk), (mv, omv), (mr, omr), (cmk, ocmk), (cmr, ocmr)):
        b.op("vector", lambda e, mvv=mvv, om=om: e.tensor_scalar(
            om[:], mvv[:], -1.0, 1.0, ALU.mult, ALU.add))
    m_exp_td = b.op("scalar", lambda e: e.activation(wdec[:], sc8[:], AF.Exp), waits=pv)
    m_neg = b.op("vector", lambda e: e.tensor_scalar(wdec[:], wdec[:], -1.0, None, ALU.mult),
                 waits=[m_exp_td])
    m_ew = b.op("scalar", lambda e: e.activation(ew[:], wdec[:], AF.Exp), waits=[m_neg])
    m_ecw = b.op("scalar", lambda e: e.activation(ecw[:], wdec[:], AF.Exp, scale=float(TCA)))
    ew03 = ew0[:].rearrange("p (i c) -> p i c", i=8)
    ejw3 = ejw[:].rearrange("p (i c) -> p i c", i=8)
    b.op("gpsimd", lambda e: e.memset(onesB[:], 1.0))
    m_iden = b.op("gpsimd", lambda e: e.affine_select(
        idenB[:], onesB[:], [[1, 128]], ALU.is_equal, 0.0, base=0, channel_multiplier=-1))
    m_iota = b.op("gpsimd", lambda e: e.iota(ejw3, [[0, 8], [1, TCA]], base=0,
                                             channel_multiplier=0,
                                             allow_small_or_imprecise_dtypes=True))
    m_jw = b.op("vector", lambda e: e.tensor_tensor(
        ejw3, ejw3, wdec[:].unsqueeze(2).broadcast_to([P, 8, TCA]), ALU.mult),
        waits=[m_iota, m_neg])
    m_ejw = b.op("scalar", lambda e: e.activation(ejw[:], ejw[:], AF.Exp), waits=[m_jw])
    m_ew0a = b.op("vector", lambda e: e.tensor_copy(
        ew03, ew[:].unsqueeze(2).broadcast_to([P, 8, TCA])), waits=[m_ew])
    m_ew0b = b.op("vector", lambda e: e.memset(ew03[:, :, 0:1], 0.0))
    m_rwc = b.op("vector", lambda e: e.tensor_copy(rwT[:], rwTf[:].transpose([0, 2, 1])))
    for t_ in (aa, bb):
        b.op("vector", lambda e, t_=t_: e.memset(t_[:], 0.0))
    b.op("vector", lambda e: e.memset(pst[:], P_INIT))
    b.op("vector", lambda e: e.memset(carry1[:], 0.0))
    m_init = b.op("vector", lambda e: e.memset(carry2[:], 0.0))

    prog = {"xn_rows": 0, "s_rows": 0, "expert_cast": 0, "g_rows": 0,
            "xn_sp": [], "s_sp": [], "g_sp": []}

    def ln_tile(src3, tt, ecol, out_bf, waits0):
        o = 6 * tt
        sqs = flat(ecb)[:, 0:1024]
        b.op("vector", lambda e, o=o, tt=tt: e.tensor_reduce(
            stat[:, o:o + 1].unsqueeze(2), src3[:, tt:tt + 1, :], AX.X, ALU.add),
            waits=waits0)
        b.op("vector", lambda e, tt=tt, sqs=sqs: e.tensor_tensor(
            sqs, src3[:, tt, :], src3[:, tt, :], ALU.mult))
        b.op("vector", lambda e, o=o, sqs=sqs: e.tensor_reduce(
            stat[:, o + 1:o + 2], sqs, AX.X, ALU.add))
        b.op("vector", lambda e, o=o: e.tensor_scalar(
            stat[:, o + 2:o + 3], stat[:, o:o + 1], 1.0 / E, None, ALU.mult))
        b.op("vector", lambda e, o=o: e.tensor_tensor(
            stat[:, o + 4:o + 5], stat[:, o + 2:o + 3], stat[:, o + 2:o + 3], ALU.mult))
        b.op("vector", lambda e, o=o: e.tensor_scalar(
            stat[:, o + 3:o + 4], stat[:, o + 1:o + 2], 1.0 / E, None, ALU.mult))
        m7 = b.op("vector", lambda e, o=o: e.tensor_tensor(
            stat[:, o + 3:o + 4], stat[:, o + 3:o + 4], stat[:, o + 4:o + 5], ALU.subtract))
        b.op("scalar", lambda e, o=o, ecol=ecol: e.activation(
            stat[:, o + 3:o + 4], stat[:, o + 3:o + 4], AF.Ln, bias=epsT[:, ecol:ecol + 1]),
            waits=[m7])
        m9 = b.op("scalar", lambda e, o=o: e.activation(
            stat[:, o + 5:o + 6], stat[:, o + 3:o + 4], AF.Exp, scale=-0.5))
        return b.op("vector", lambda e, o=o, tt=tt, out_bf=out_bf: e.tensor_scalar(
            out_bf[:, tt, :], src3[:, tt, :], stat[:, o + 2:o + 3], stat[:, o + 5:o + 6],
            ALU.subtract, ALU.mult), waits=[m9])

    def transpose_in(src_bf, dst, gvec, bvec, src_ms):
        mh = None
        for tt in range(2):
            bank = AP_[tt % 2]
            psbf = PS[bank][:].bitcast(BF16)
            for i in range(8):
                wts_ = []
                if i == 0:
                    wts_ = [m_iden, src_ms] + \
                           ([psum_free[bank]] if psum_free[bank] else [])
                b.mm(lambda e, i=i, tt=tt, psbf=psbf, src_bf=src_bf: e.transpose(
                    psbf[:, 128 * i:128 * (i + 1)],
                    src_bf[:, tt, 128 * i:128 * (i + 1)], idenB[:]),
                    waits=wts_, mark=(i == 7))
            gm = ("pe", b.c["pe"])
            for i in range(8):
                mh = b.op("vector", lambda e, i=i, tt=tt, psbf=psbf, dst=dst,
                          gvec=gvec, bvec=bvec: e.tensor_scalar(
                    dst[:, i, 128 * tt:128 * (tt + 1)], psbf[:, 128 * i:128 * (i + 1)],
                    gvec[:, i:i + 1], bvec[:, i:i + 1], ALU.mult, ALU.add),
                    waits=[gm] if i == 0 else [])
            psum_free[bank] = mh
        return mh, ("pe", b.c["pe"])

    def gen_A():
        last_x0_use = None
        ld_x0 = None
        last_s_rd = []
        kc_free = None
        t1b_free = None
        wslot_free = [None] * 4
        rslot_free = [None] * 4
        cast_i = 0
        dbg_xn = None
        dbg_wkv = None
        fc_st = None
        fc_m1 = None
        fc_m2 = None
        fc_next = 0

        def emit_fc(cc, base_mark):
            nonlocal fc_st, fc_m1, fc_m2
            r0f = cc * TCA
            g_w = prog["g_sp"][cc // 2]
            fx = flat(kT).rearrange("p (t e) -> p t e", t=2)
            fs = flat(sab).rearrange("p (t e) -> p t e", t=2)
            fgv = flat(vT).rearrange("p (t e) -> p t e", t=2)
            l1 = b.dma("gpsimd", lambda e, r0f=r0f, fx=fx: e.dma_start(
                fx, rearr_rows(I["x"], r0f, TCA)),
                waits=[base_mark, dbg_wkv, fc_st])
            l2 = b.dma("gpsimd", lambda e, r0f=r0f, fs=fs: e.dma_start(
                fs, rearr_rows(S["s_f"], r0f, TCA)),
                waits=[base_mark, fc_m2, prog["s_sp"][cc]])
            l3 = b.dma("gpsimd", lambda e, r0f=r0f, fgv=fgv: e.dma_start(
                fgv, rearr_rows(S["g_b"], r0f, TCA)),
                waits=[base_mark, fc_m1, g_w])
            fc_m1 = b.op("vector", lambda e, fs=fs, fgv=fgv: e.tensor_tensor(
                flat(fs), flat(fs), flat(fgv), ALU.mult), waits=[l2, l3])
            fc_m2 = b.op("vector", lambda e, fx=fx, fs=fs: e.tensor_tensor(
                flat(fx), flat(fx), flat(fs), ALU.add), waits=[l1])
            fc_st = b.dma("gpsimd", lambda e, r0f=r0f, fx=fx: e.dma_start(
                rearr_rows(y, r0f, TCA), fx), waits=[fc_m2])

        def row_stream(lhsT_src, w_dram, out_banks4, cast_ms, first_wait,
                       kk_base=0, n_kk=8, stop_kk=None):
            res = []
            nonlocal_rs = rslot_free
            if stop_kk is None:
                stop_kk = n_kk - 1
            for nh in range(2):
                pw = {}
                for m8 in range(2):
                    bank = out_banks4[2 * nh + m8]
                    pw[m8] = [psum_free[bank]] if psum_free[bank] else []
                for kk in range(n_kk):
                    slot = kk % 4
                    ldr = b.dma("sync", lambda e, kk=kk, nh=nh, slot=slot,
                                w_dram=w_dram, kk_base=kk_base: e.dma_start(
                        wrow[slot][:], w_dram.rearrange("(kk p) n -> p kk n", p=P)[
                            :, kk_base + kk, 512 * nh:512 * (nh + 1)]),
                        waits=[cast_ms, nonlocal_rs[slot]])
                    for m8 in range(2):
                        bank = out_banks4[2 * nh + m8]
                        wts_ = []
                        if kk == 0:
                            wts_ = [first_wait] + pw[m8]
                        if m8 == 0:
                            wts_.append(ldr)
                        b.mm(lambda e, kk=kk, m8=m8, nh=nh, bank=bank, slot=slot,
                             lhsT_src=lhsT_src: e.matmul(
                            PS[bank][:], lhsT_src[:, kk, 128 * m8:128 * (m8 + 1)],
                            wrow[slot][:], start=(kk == 0), stop=(kk == stop_kk)),
                            waits=wts_, mark=(m8 == 1))
                        if m8 == 1:
                            nonlocal_rs[slot] = ("pe", b.c["pe"])
                for m8 in range(2):
                    res.append((("pe", b.c["pe"]), out_banks4[2 * nh + m8], m8, nh))
            return res

        for c in range(NCA):
            r0 = c * TCA
            for _ in range(2):
                if cast_i < len(expert_casts):
                    ei, kind = expert_casts[cast_i]
                    src = I["w1"][ei] if kind == "bw1" else I["w2"][ei]
                    dst = S[kind][ei]
                    cast_done[(kind, ei)] = b.dma(
                        "gpsimd", lambda e, s_=src, d=dst: e.dma_start(d, s_))
                    cast_i += 1
                    prog["expert_cast"] = cast_i
            if ld_x0 is None:
                ld_x0 = b.dma("sync", lambda e, r0=r0: e.dma_start(
                    x0c, rearr_rows(I["x"], r0, TCA)), waits=[last_x0_use, dbg_xn])
            yield 2
            xn_ms = []
            for tt in range(2):
                w0 = [ld_x0]
                if tt == 0:
                    w0 += last_s_rd + ([m_init] if c == 0 else [])
                xn_ms.append(ln_tile(x0c, tt, 0, xnc, w0))
            last_x0_use = xn_ms[-1]
            if c + 1 < NCA:
                r1 = (c + 1) * TCA
                ld_x0 = b.dma("sync", lambda e, r1=r1: e.dma_start(
                    x0c, rearr_rows(I["x"], r1, TCA)), waits=[last_x0_use, dbg_xn])
            else:
                ld_x0 = None
            yield 16
            xnb = flat(hsh).rearrange("p (t e) -> p t e", t=2)
            mcast = None
            for tt in range(2):
                mcast = b.op("vector", lambda e, tt=tt, xnb=xnb: e.tensor_copy(
                    xnb[:, tt, :], xnc[:, tt, :]))
            sp_xn = b.dma("sync", lambda e, r0=r0, xnb=xnb: e.dma_start(
                rearr_rows(S["xn_b"], r0, TCA), xnb), waits=[mcast])
            prog["xn_sp"].append(sp_xn)
            if _DEBUG:
                dbg_xn = b.dma("sync", lambda e, r0=r0: e.dma_start(
                    rearr_rows(dbg["d_xn"], r0, TCA), xnc), waits=[xn_ms[-1]])
            prog["xn_rows"] = r0 + TCA
            yield 3
            t1_ms = [ln_tile(xnc, tt, 1, t1b,
                             [t1b_free] if tt == 0 and t1b_free else [])
                     for tt in range(2)]
            yield 14
            _, t1b_free = transpose_in(t1b, hT, g1, b1, t1_ms[-1])
            yield 6
            b.op("vector", lambda e: e.tensor_copy(
                hsh[:, :, 1:TCA], hT[:, :, 0:TCA - 1]), waits=[sp_xn])
            m_s2 = b.op("vector", lambda e: e.tensor_copy(
                hsh[:, :, 0:1], carry1[:].unsqueeze(2)))
            b.op("vector", lambda e: e.tensor_copy(
                carry1[:].unsqueeze(2), hT[:, :, TCA - 1:TCA]))
            yield 3
            evac_ms = {}
            mix_v = {"k": (mk, omk), "v": (mv, omv), "r": (mr, omr)}
            wdram = {"k": S["bWk"], "v": S["bWv"], "r": S["bWr"]}
            wcast = {"k": cast_done["bWk"], "v": cast_done["bWv"], "r": cast_done["bWr"]}
            for pi, pn in enumerate(("k", "v", "r")):
                xm = xm0 if pi % 2 == 0 else xm1
                mvec, ovec = mix_v[pn]
                mmix0 = b.op("vector", lambda e, xm=xm, ovec=ovec: e.tensor_tensor(
                    xm[:], hsh[:], ovec[:].unsqueeze(2).broadcast_to([P, 8, TCA]),
                    ALU.mult), waits=[m_s2, ("pe", b.c["pe"])])
                mmix = None
                for i in range(8):
                    mmix = b.op("vector", lambda e, i=i, xm=xm, mvec=mvec:
                                e.scalar_tensor_tensor(
                        xm[:, i, :], hT[:, i, :], mvec[:, i:i + 1], xm[:, i, :],
                        ALU.mult, ALU.add))
                yield 5
                last_ev = None
                for m8 in range(8):
                    slot = m8 % 4
                    ldw = b.dma("sync", lambda e, pn=pn, m8=m8, slot=slot: e.dma_start(
                        wtile[slot][:], wdram[pn][:, 128 * m8:128 * (m8 + 1)].rearrange(
                            "(kk p) n -> p kk n", p=P)),
                        waits=[wcast[pn], wslot_free[slot]])
                    bank = (AB + AP_)[m8 % 4]
                    wts_ = [ldw]
                    if m8 == 0:
                        wts_.append(mmix)
                    if psum_free[bank]:
                        wts_.append(psum_free[bank])
                    for kk in range(8):
                        b.mm(lambda e, kk=kk, slot=slot, bank=bank, xm=xm: e.matmul(
                            PS[bank][:, 0:TCA], wtile[slot][:, kk, :], xm[:, kk, :],
                            start=(kk == 0), stop=(kk == 7)),
                            waits=wts_ if kk == 0 else [], mark=(kk == 7))
                    gm = ("pe", b.c["pe"])
                    wslot_free[slot] = gm
                    if pn == "k":
                        ev = b.op("vector", lambda e, m8=m8, bank=bank: e.tensor_copy(
                            kT[:, m8, :], PS[bank][:, 0:TCA]),
                            waits=[gm, dbg_wkv, fc_st] if m8 == 0 else [gm])
                    elif pn == "v":
                        ev = b.op("vector", lambda e, m8=m8, bank=bank: e.tensor_copy(
                            vT[:, m8, :], PS[bank][:, 0:TCA]), waits=[gm])
                    else:
                        ev = b.op("scalar", lambda e, m8=m8, bank=bank: e.activation(
                            xm1[:, m8, :], PS[bank][:, 0:TCA], AF.Sigmoid), waits=[gm])
                    psum_free[bank] = ev
                    last_ev = ev
                    if m8 % 2 == 1:
                        yield 2.2
                evac_ms[pn] = last_ev
            dbg_k = None
            if _DEBUG:
                dbg_k = b.dma("sync", lambda e, r0=r0: e.dma_start(
                    dbg["d_k"].rearrange("(i p) t -> p i t", p=P)[:, :, r0:r0 + TCA], kT),
                    waits=[evac_ms["k"]])
            m_ng = b.op("vector", lambda e: e.tensor_reduce(
                ng[:].unsqueeze(2), kT, AX.X, ALU.max, negate=True), waits=[evac_ms["k"]])
            m_ug = b.op("vector", lambda e: e.tensor_tensor(ug[:], uvec[:], ng[:], ALU.add))
            m_ec = None
            for i in range(8):
                m_ec = b.op("scalar", lambda e, i=i: e.activation(
                    ecb[:, i, :], kT[:, i, :], AF.Exp, bias=ng[:, i:i + 1]),
                    waits=([m_ng] if i == 0 else []))
            m_za = b.op("vector", lambda e: e.tensor_tensor(zab, ecb, vT[:], ALU.mult),
                        waits=[m_ec, evac_ms["v"]])
            m_sa = b.op("vector", lambda e: e.tensor_tensor_scan(
                flat(sab), ew0[:], flat(zab), 0.0, ALU.mult, ALU.add),
                waits=[m_ew0b, m_ew0a])
            m_sb = b.op("vector", lambda e: e.tensor_tensor_scan(
                flat(zab), ew0[:], flat(ecb), 0.0, ALU.mult, ALU.add))
            yield 12
            m_t = b.op("vector", lambda e: e.tensor_tensor(sc8[:], pst[:], ng[:], ALU.add))
            m_eP = b.op("scalar", lambda e: e.activation(sc8[:], sc8[:], AF.Exp),
                        waits=[m_t])
            b.op("vector", lambda e: e.tensor_tensor(aag[:], aa[:], sc8[:], ALU.mult),
                 waits=[m_eP])
            b.op("vector", lambda e: e.tensor_tensor(bbg[:], bb[:], sc8[:], ALU.mult))
            b.op("vector", lambda e: e.tensor_tensor(aa[:], ecw[:], aag[:], ALU.mult),
                 waits=[m_ecw])
            b.op("vector", lambda e: e.tensor_tensor(
                aa[:].unsqueeze(2), aa[:].unsqueeze(2), sab[:, :, TCA - 1:TCA], ALU.add))
            b.op("vector", lambda e: e.tensor_tensor(bb[:], ecw[:], bbg[:], ALU.mult))
            b.op("vector", lambda e: e.tensor_tensor(
                bb[:].unsqueeze(2), bb[:].unsqueeze(2), zab[:, :, TCA - 1:TCA], ALU.add))
            b.op("vector", lambda e: e.tensor_scalar(
                pst[:], ng[:], -1.0, None, ALU.mult))
            m_es = None
            for i in range(8):
                m_es = b.op("scalar", lambda e, i=i: e.activation(
                    ecb[:, i, :], kT[:, i, :], AF.Exp, bias=ug[:, i:i + 1]),
                    waits=([m_sb, m_ug] if i == 0 else []))
            yield 8
            b.op("vector", lambda e: e.tensor_tensor(
                kT, ejw3, aag[:].unsqueeze(2).broadcast_to([P, 8, TCA]), ALU.mult),
                waits=[m_es, m_ejw, dbg_k])
            b.op("vector", lambda e: e.tensor_tensor(
                kT[:, :, 1:], kT[:, :, 1:], sab[:, :, 0:TCA - 1], ALU.add))
            b.op("vector", lambda e: e.tensor_tensor(vT[:], ecb, vT[:], ALU.mult))
            b.op("vector", lambda e: e.tensor_tensor(kT, kT, vT[:], ALU.add))
            b.op("vector", lambda e: e.tensor_tensor(
                sab, ejw3, bbg[:].unsqueeze(2).broadcast_to([P, 8, TCA]), ALU.mult))
            b.op("vector", lambda e: e.tensor_tensor(
                sab[:, :, 1:], sab[:, :, 1:], zab[:, :, 0:TCA - 1], ALU.add))
            m_d3 = b.op("vector", lambda e: e.tensor_tensor(sab, sab, ecb, ALU.add))
            b.op("scalar", lambda e: e.activation(flat(sab), flat(sab), AF.Ln),
                 waits=[m_d3])
            m_r2 = b.op("scalar", lambda e: e.activation(flat(sab), flat(sab), AF.Exp,
                                                         scale=-1.0))
            m_wkv = b.op("vector", lambda e: e.tensor_tensor(kT, kT, sab, ALU.mult),
                         waits=[m_r2])
            if _DEBUG:
                dbg_wkv = b.dma("sync", lambda e, r0=r0: e.dma_start(
                    dbg["d_wkv"].rearrange("(i p) t -> p i t", p=P)[:, :, r0:r0 + TCA], kT),
                    waits=[m_wkv])
            m_rk = b.op("vector", lambda e: e.tensor_tensor(xm0[:], kT, xm1[:], ALU.mult),
                        waits=[evac_ms["r"], ("pe", b.c["pe"])])
            last_mrk = m_rk
            yield 16
            groups = row_stream(xm0, S["bWo"], AB + AP_, cast_done["bWo"], m_rk)
            x1_ms = []
            for (gm_, bank, m8, nh) in groups:
                ev = b.op("vector", lambda e, m8=m8, nh=nh, bank=bank: e.tensor_tensor(
                    xnc[:, m8, 512 * nh:512 * (nh + 1)],
                    xnc[:, m8, 512 * nh:512 * (nh + 1)], PS[bank][:], ALU.add),
                    waits=[gm_])
                psum_free[bank] = ev
                x1_ms.append(ev)
            yield 8
            drained = 0
            while (fc_next <= c - 2 and drained < 2
                   and prog["g_rows"] >= fc_next * TCA + TCA):
                emit_fc(fc_next, m_rk)
                fc_next += 1
                drained += 1
                yield 5
            if _DEBUG:
                b.dma("sync", lambda e, r0=r0: e.dma_start(
                    rearr_rows(dbg["d_x1"], r0, TCA), xnc), waits=[x1_ms[-1]])
            t2_ms = [ln_tile(xnc, tt, 1, t1b,
                             [x1_ms[-1], t1b_free] if tt == 0 else [])
                     for tt in range(2)]
            yield 14
            _, t1b_free = transpose_in(t1b, hT, g2, b2, t2_ms[-1])
            yield 6
            b.op("vector", lambda e: e.tensor_copy(hsh[:, :, 1:TCA], hT[:, :, 0:TCA - 1]))
            m2_s2 = b.op("vector", lambda e: e.tensor_copy(
                hsh[:, :, 0:1], carry2[:].unsqueeze(2)))
            b.op("vector", lambda e: e.tensor_copy(
                carry2[:].unsqueeze(2), hT[:, :, TCA - 1:TCA]))
            yield 3
            b.op("vector", lambda e: e.tensor_tensor(
                xm1[:], hsh[:], ocmr[:].unsqueeze(2).broadcast_to([P, 8, TCA]), ALU.mult),
                waits=[m2_s2, ("pe", b.c["pe"])])
            mmxr = None
            for i in range(8):
                mmxr = b.op("vector", lambda e, i=i: e.scalar_tensor_tensor(
                    xm1[:, i, :], hT[:, i, :], cmr[:, i:i + 1], xm1[:, i, :],
                    ALU.mult, ALU.add))
            yield 5
            groups = row_stream(xm1, S["bcWr"], AB + AP_, cast_done["bcWr"], mmxr)
            for (gm_, bank, m8, nh) in groups:
                ev = b.op("scalar", lambda e, m8=m8, nh=nh, bank=bank: e.activation(
                    rc2[:, m8, 512 * nh:512 * (nh + 1)], PS[bank][:], AF.Sigmoid),
                    waits=[gm_])
                psum_free[bank] = ev
            rc_mark = ("pe", b.c["pe"])
            yield 8
            b.op("vector", lambda e: e.tensor_tensor(
                xm1[:], hsh[:], ocmk[:].unsqueeze(2).broadcast_to([P, 8, TCA]), ALU.mult),
                waits=[rc_mark])
            mmxk = None
            for i in range(8):
                mmxk = b.op("vector", lambda e, i=i: e.scalar_tensor_tensor(
                    xm1[:, i, :], hT[:, i, :], cmk[:, i:i + 1], xm1[:, i, :],
                    ALU.mult, ALU.add))
            yield 5
            s_ms = []
            rel_free = [None, None]
            for fh in range(2):
                sq_ms = None
                for fm in range(16):
                    fmg = 16 * fh + fm
                    slot = fm % 4
                    ldt = b.dma("sync", lambda e, fmg=fmg, slot=slot: e.dma_start(
                        wtile[slot][:], S["bcWk"][:, 128 * fmg:128 * (fmg + 1)].rearrange(
                            "(kk p) n -> p kk n", p=P)),
                        waits=[cast_done["bcWk"], wslot_free[slot]])
                    bank = AP_[fm % 2]
                    wts_ = [ldt]
                    if fm == 0:
                        wts_.append(mmxk)
                        if fh == 0 and kc_free:
                            wts_.append(kc_free)
                    if psum_free[bank]:
                        wts_.append(psum_free[bank])
                    for kk in range(8):
                        b.mm(lambda e, kk=kk, slot=slot, bank=bank: e.matmul(
                            PS[bank][:, 0:TCA], wtile[slot][:, kk, :], xm1[:, kk, :],
                            start=(kk == 0), stop=(kk == 7)),
                            waits=wts_ if kk == 0 else [], mark=(kk == 7))
                    gm = ("pe", b.c["pe"])
                    wslot_free[slot] = gm
                    rel = relA if fm % 2 == 0 else relB
                    evr = b.op("scalar", lambda e, bank=bank, rel=rel: e.activation(
                        rel[:], PS[bank][:, 0:TCA], AF.Relu),
                        waits=[gm, rel_free[fm % 2]])
                    psum_free[bank] = evr
                    sq_ms = b.op("vector", lambda e, fm=fm, rel=rel: e.tensor_tensor(
                        kcH[:, fm, :], rel[:], rel[:], ALU.mult), waits=[evr])
                    rel_free[fm % 2] = sq_ms
                    if fm % 2 == 1:
                        yield 2.2
                groups = row_stream(kcH, S["bcWv"], AB + AP_, cast_done["bcWv"], sq_ms,
                                    kk_base=16 * fh, n_kk=16, stop_kk=15)
                for (gm_, bank, m8, nh) in groups:
                    mt = b.op("vector", lambda e, m8=m8, nh=nh, bank=bank: e.tensor_tensor(
                        tmp5[:], PS[bank][:], rc2[:, m8, 512 * nh:512 * (nh + 1)],
                        ALU.mult), waits=[gm_])
                    ma = b.op("vector", lambda e, m8=m8, nh=nh: e.tensor_tensor(
                        xnc[:, m8, 512 * nh:512 * (nh + 1)],
                        xnc[:, m8, 512 * nh:512 * (nh + 1)], tmp5[:], ALU.add))
                    psum_free[bank] = ma
                    s_ms.append(ma)
                kc_free = ("pe", b.c["pe"])
                yield 14
            sp_s = b.dma("sync", lambda e, r0=r0: e.dma_start(
                rearr_rows(S["s_f"], r0, TCA), xnc), waits=[s_ms[-1]])
            last_s_rd = [sp_s]
            if _DEBUG:
                last_s_rd.append(b.dma("sync", lambda e, r0=r0: e.dma_start(
                    rearr_rows(dbg["d_s"], r0, TCA), xnc), waits=[s_ms[-1]]))
            prog["s_sp"].append(sp_s)
            prog["s_rows"] = r0 + TCA
            yield 3
        while fc_next < NCA:
            while prog["g_rows"] < fc_next * TCA + TCA:
                yield ("stall",)
            emit_fc(fc_next, last_mrk)
            fc_next += 1
            yield 5

    def gen_F():
        last_xnT = None
        last_gout = None
        last_gacc = None
        w2_free = [None, None]
        w1_free = [None, None, None, None]

        for cf in range(NCF):
            r0 = cf * TCF
            while prog["xn_rows"] < r0 + TCF:
                yield ("stall",)
            xn_sp_w = [prog["xn_sp"][2 * cf], prog["xn_sp"][2 * cf + 1]]
            xnT_lds = []
            for i in range(8):
                xnT_lds.append(b.dma("scalar", lambda e, r0=r0, i=i: e.dma_start(
                    xnT[:, i, :], S["xn_b"][r0:r0 + TCF, 128 * i:128 * (i + 1)],
                    transpose=True),
                    waits=([last_xnT, last_gout] + xn_sp_w) if i == 0 else []))
            yield 8
            lg = None
            for m8 in range(4):
                bank = FC_[0]
                wts_ = (xnT_lds + [m_rwc]) if m8 == 0 else []
                if psum_free[bank]:
                    wts_ = wts_ + [psum_free[bank]]
                for kk in range(8):
                    b.mm(lambda e, m8=m8, kk=kk, bank=bank: e.matmul(
                        PS[bank][:, 0:NE], xnT[:, kk, 128 * m8:128 * (m8 + 1)],
                        rwT[:, kk, :], start=(kk == 0), stop=(kk == 7)),
                        waits=wts_, mark=(kk == 7))
                gm = ("pe", b.c["pe"])
                lg = b.op("vector", lambda e, m8=m8, bank=bank: e.tensor_copy(
                    logit[:, m8, :], PS[bank][:, 0:NE]), waits=[gm])
                psum_free[bank] = lg
            yield 4
            b.op("vector", lambda e: e.tensor_reduce(
                m1n[:].unsqueeze(2), logit[:], AX.X, ALU.max, negate=True), waits=[lg])
            b.op("vector", lambda e: e.tensor_tensor(
                tkA[:], logit[:], m1n[:].unsqueeze(2).broadcast_to([P, 4, NE]), ALU.add))
            b.op("vector", lambda e: e.tensor_scalar(
                tkB[:], tkA[:], 0.0, None, ALU.is_equal))
            b.op("vector", lambda e: e.scalar_tensor_tensor(
                tkA[:], tkB[:], -1e30, logit[:], ALU.mult, ALU.add))
            b.op("vector", lambda e: e.tensor_reduce(
                m2n[:].unsqueeze(2), tkA[:], AX.X, ALU.max, negate=True))
            b.op("vector", lambda e: e.tensor_tensor(
                tkA[:], tkA[:], m2n[:].unsqueeze(2).broadcast_to([P, 4, NE]), ALU.add))
            b.op("vector", lambda e: e.tensor_scalar(
                tkA[:], tkA[:], 0.0, None, ALU.is_equal))
            mH = b.op("vector", lambda e: e.tensor_tensor(tkB[:], tkB[:], tkA[:], ALU.add))
            mI = None
            for m8 in range(4):
                mI = b.op("scalar", lambda e, m8=m8: e.activation(
                    wts[:, m8, :], logit[:, m8, :], AF.Exp, bias=m1n[:, m8:m8 + 1]),
                    waits=([mH] if m8 == 0 else []))
            b.op("vector", lambda e: e.tensor_tensor(wts[:], wts[:], tkB[:], ALU.mult),
                 waits=[mI])
            b.op("vector", lambda e: e.tensor_reduce(
                zsum[:].unsqueeze(2), wts[:], AX.X, ALU.add))
            b.op("vector", lambda e: e.reciprocal(zsum[:], zsum[:]))
            mW = b.op("vector", lambda e: e.tensor_tensor(
                wts[:], wts[:], zsum[:].unsqueeze(2).broadcast_to([P, 4, NE]), ALU.mult))
            mz = b.op("vector", lambda e: e.memset(flat(gacc), 0.0), waits=[last_gacc])
            yield 6
            for ei in range(NE):
                while prog["expert_cast"] < 2 * (ei + 1):
                    yield ("stall",)
                for h in range(2):
                    slot2 = h
                    w2ld = b.dma("scalar", lambda e, ei=ei, h=h, slot2=slot2: e.dma_start(
                        w2s[slot2][:], S["bw2"][ei][1024 * h:1024 * (h + 1), :].rearrange(
                            "(kk p) n -> p kk n", p=P)),
                        waits=[cast_done[("bw2", ei)], w2_free[slot2]])
                    he_last = None
                    w1_lds = {}

                    def emit_w1(m8, ei=ei, h=h, w1_lds=w1_lds):
                        slot = m8 % 4
                        col = 1024 * h + 128 * m8
                        w1_lds[m8] = b.dma("scalar", lambda e, ei=ei, col=col, slot=slot:
                                           e.dma_start(
                            w1t[slot][:], S["bw1"][ei][:, col:col + 128].rearrange(
                                "(kk p) n -> p kk n", p=P)),
                            waits=[cast_done[("bw1", ei)], w1_free[slot]])

                    emit_w1(0)
                    emit_w1(1)
                    for m8 in range(8):
                        slot = m8 % 4
                        bank = FB[m8 % 2]
                        wts_ = [w1_lds[m8]]
                        if psum_free[bank]:
                            wts_.append(psum_free[bank])
                        for kk in range(8):
                            b.mm(lambda e, kk=kk, slot=slot, bank=bank: e.matmul(
                                PS[bank][:], w1t[slot][:, kk, :], xnT[:, kk, :],
                                start=(kk == 0), stop=(kk == 7)),
                                waits=wts_ if kk == 0 else [], mark=(kk == 7))
                            if kk == 0 and m8 + 2 < 8:
                                emit_w1(m8 + 2)
                        gm = ("pe", b.c["pe"])
                        w1_free[slot] = gm
                        he_last = b.op("scalar", lambda e, m8=m8, bank=bank: e.activation(
                            heT[:, m8, :], PS[bank][:], AF.Gelu), waits=[gm])
                        psum_free[bank] = he_last
                        yield 1.9
                    for m8 in range(4):
                        for nh in range(2):
                            bank = FC_[nh % 2]
                            wts_ = []
                            if m8 == 0 and nh == 0:
                                wts_ = [he_last, w2ld]
                            if psum_free[bank]:
                                wts_ = wts_ + [psum_free[bank]]
                            for kk2 in range(8):
                                b.mm(lambda e, kk2=kk2, m8=m8, nh=nh, slot2=slot2,
                                     bank=bank: e.matmul(
                                    PS[bank][:], heT[:, kk2, 128 * m8:128 * (m8 + 1)],
                                    w2s[slot2][:, kk2, 512 * nh:512 * (nh + 1)],
                                    start=(kk2 == 0), stop=(kk2 == 7)),
                                    waits=wts_, mark=(kk2 == 7))
                            gm = ("pe", b.c["pe"])
                            ev = b.op("vector", lambda e, m8=m8, nh=nh, ei=ei, bank=bank:
                                      e.scalar_tensor_tensor(
                                gacc[:, m8, 512 * nh:512 * (nh + 1)], PS[bank][:],
                                wts[:, m8, ei:ei + 1],
                                gacc[:, m8, 512 * nh:512 * (nh + 1)],
                                ALU.mult, ALU.add),
                                waits=[gm] + ([mW, mz] if m8 == 0 and nh == 0 else []))
                            psum_free[bank] = ev
                            yield 1.9
                    w2_free[slot2] = ("pe", b.c["pe"])
            gdone = ("v", b.c["v"])
            m_g = b.op("scalar", lambda e: e.activation(flat(gout), flat(gacc), AF.Silu),
                       waits=[gdone, last_gout])
            last_gacc = m_g
            sp_g = b.dma("scalar", lambda e, r0=r0: e.dma_start(
                rearr_rows(S["g_b"], r0, TCF), gout), waits=[m_g])
            last_gout = sp_g
            last_xnT = ("pe", b.c["pe"])
            dbg_g = None
            if _DEBUG:
                dbf = b.op("vector", lambda e: e.tensor_copy(flat(gacc), flat(gout)),
                           waits=[m_g])
                dbg_g = b.dma("sync", lambda e, r0=r0: e.dma_start(
                    rearr_rows(dbg["d_g"], r0, TCF), gacc), waits=[dbf])
                last_gacc = dbg_g
            yield 4
            prog["g_sp"].append(sp_g)
            prog["g_rows"] = r0 + TCF
            last_gacc = m_g if not _DEBUG else last_gacc

    ag = gen_A()
    fg = gen_F()
    tA = 0.0
    tF = 0.0
    a_alive = f_alive = True
    a_stalled = f_stalled = False
    while a_alive or f_alive:
        if a_stalled and f_stalled:
            raise RuntimeError("driver deadlock: both streams stalled")
        if not f_alive:
            pull_a = True
        elif not a_alive:
            pull_a = False
        elif f_stalled:
            pull_a = True
        elif a_stalled:
            pull_a = False
        else:
            pull_a = tA + 60.0 <= tF
        if pull_a:
            try:
                r = next(ag)
                if isinstance(r, tuple):
                    a_stalled = True
                else:
                    tA += r
                    f_stalled = False
            except StopIteration:
                a_alive = False
                tA = float("inf")
                f_stalled = False
        else:
            try:
                r = next(fg)
                if isinstance(r, tuple):
                    f_stalled = True
                else:
                    tF += r
                    a_stalled = False
            except StopIteration:
                f_alive = False
                tF = float("inf")
                a_stalled = False

    fin = (b.queue_done_waits("sync") + b.queue_done_waits("scalar")
           + b.queue_done_waits("gpsimd"))

    def _final(e, fin=fin):
        for (k, v) in fin:
            e.wait_ge(sems[k], v)
    b.raw("sync", _final)

    with nc.Block() as block:
        block.sync(lambda e: [f(e) for f in b.q["sync"]])
        block.gpsimd(lambda e: [f(e) for f in b.q["gpsimd"]])
        block.scalar(lambda e: [f(e) for f in b.q["scalar"]])
        block.vector(lambda e: [f(e) for f in b.q["vector"]])
        block.tensor(lambda e: [f(e) for f in b.q["tensor"]])

    for cm in reversed(sem_cms):
        cm.__exit__(None, None, None)
    return nc


_NC_CACHE = None


def kernel(**inputs):
    global _NC_CACHE
    if _NC_CACHE is None:
        _NC_CACHE = build_nc()
    nc = _NC_CACHE

    x = np.ascontiguousarray(np.asarray(inputs["L0_attn_inputs"], np.float32))
    base = {
        "ln1g": np.asarray(inputs["L0_attn_ln1_g"], np.float32),
        "ln1b": np.asarray(inputs["L0_attn_ln1_b"], np.float32),
        "ln2g": np.asarray(inputs["L0_attn_ln2_g"], np.float32),
        "ln2b": np.asarray(inputs["L0_attn_ln2_b"], np.float32),
        "mixk": np.asarray(inputs["L0_attn_tm_mix_k"], np.float32),
        "mixv": np.asarray(inputs["L0_attn_tm_mix_v"], np.float32),
        "mixr": np.asarray(inputs["L0_attn_tm_mix_r"], np.float32),
        "cmixk": np.asarray(inputs["L0_attn_cm_mix_k"], np.float32),
        "cmixr": np.asarray(inputs["L0_attn_cm_mix_r"], np.float32),
        "td": np.asarray(inputs["L0_attn_time_decay"], np.float32),
        "tf": np.asarray(inputs["L0_attn_time_first"], np.float32),
        "Wk": np.asarray(inputs["L0_attn_tm_Wk"], np.float32),
        "Wv": np.asarray(inputs["L0_attn_tm_Wv"], np.float32),
        "Wr": np.asarray(inputs["L0_attn_tm_Wr"], np.float32),
        "Wo": np.asarray(inputs["L0_attn_Wo"], np.float32),
        "cWk": np.asarray(inputs["L0_attn_cm_Wk"], np.float32),
        "cWv": np.asarray(inputs["L0_attn_cm_Wv"], np.float32),
        "cWr": np.asarray(inputs["L0_attn_cm_Wr"], np.float32),
        "rw": np.asarray(inputs["L0_ffn_router_w"], np.float32),
        "w1": np.asarray(inputs["L0_ffn_w1"], np.float32),
        "w2": np.asarray(inputs["L0_ffn_w2"], np.float32),
    }
    in_maps = [dict(base, x=x[bi]) for bi in range(8)]
    res = run_bass_kernel_spmd(nc, in_maps, core_ids=list(range(8)))
    out = np.stack([res.results[bi]["y"] for bi in range(8)], axis=0).astype(np.float32)
    if _DEBUG:
        kernel.debug_results = res.results
    return out
